# version 02
# baseline (speedup 1.0000x reference)
import math
import numpy as np
import ml_dtypes
import concourse.bass as bass
import concourse.mybir as mybir
from concourse.bass_utils import run_bass_kernel_spmd

F32 = mybir.dt.float32
BF16 = mybir.dt.bfloat16
AF = mybir.ActivationFunctionType
OP = mybir.AluOpType
AX = mybir.AxisListType
BF16NP = ml_dtypes.bfloat16

NEG = -3.0e38


class Geo:
    def __init__(self, T_seq=16384, B=2, n_cores=8):
        self.T_seq = T_seq
        self.B = B
        self.n_cores = n_cores
        assert (T_seq * B) % n_cores == 0
        self.cores_per_b = n_cores // B
        self.own = T_seq // self.cores_per_b
        self.W = 512
        assert self.own % self.W == 0
        self.n_own = self.own // self.W
        self.warm = 3
        self.LH = self.warm * self.W + 128
        self.RH = 128
        self.Tc = self.LH + self.own + self.RH
        self.stripes = list(range(-self.warm, self.n_own))

    def core_pos(self, c):
        b = c // self.cores_per_b
        s = (c % self.cores_per_b) * self.own
        return b, s

    def t0_off(self, k):
        return self.LH + k * self.W



def _rotary_tables(n, t_lo, t_hi):
    DIM_HEAD, WINDOW, QK_SCALE = 64, 512, 8.0
    half = np.arange(0, DIM_HEAD, 2, dtype=np.float64)
    inv_freq = 1.0 / 10000.0 ** (half / DIM_HEAD)
    t = np.arange(t_lo, t_hi, dtype=np.float64)
    freqs = t[:, None] * inv_freq[None, :]
    freqs = np.concatenate([freqs, freqs], axis=-1)
    base = (half + 0.4 * DIM_HEAD) / (1.4 * DIM_HEAD)
    power = (t - n // 2) / float(WINDOW // 2)
    scale = base[None, :] ** power[:, None]
    scale = np.concatenate([scale, scale], axis=-1)
    cos, sin = np.cos(freqs), np.sin(freqs)
    A_q = cos * scale
    B_q = sin * scale
    A_k = cos / scale
    B_k = sin / scale
    def sgn(Bt):
        Bt = Bt.copy()
        Bt[:, 0:32] *= -1.0
        return Bt
    f32 = np.float32
    return (A_q.astype(f32) * np.float32(QK_SCALE),
            sgn(B_q).astype(np.float64).astype(f32) * np.float32(QK_SCALE),
            A_k.astype(f32), sgn(B_k).astype(f32))


def _band_masks():
    kp = np.arange(128)[:, None]
    qp = np.arange(512)[None, :]
    out = np.zeros((8, 128, 512), dtype=np.float32)
    for c in range(8):
        allowed = (qp >= 128 * c + kp - 512) & (qp <= 128 * c + kp)
        out[c] = np.where(allowed, 0.0, NEG)
    return out.astype(BF16NP)


def stage_inputs(geo, inputs):
    g = geo
    x_full = np.asarray(inputs["L0_attn_query"], dtype=np.float32)
    masks8 = _band_masks()

    W_inT = np.ascontiguousarray(np.asarray(inputs["L0_attn_W_in"], np.float32).T)
    W_attnT = np.ascontiguousarray(np.asarray(inputs["L0_attn_W_attn"], np.float32).T)
    W_outT = np.ascontiguousarray(np.asarray(inputs["L0_attn_W_out"], np.float32).T)
    wqkv = np.ascontiguousarray(np.asarray(inputs["L0_ffn_wqkv"], np.float32))
    wo_h = np.ascontiguousarray(
        np.asarray(inputs["L0_ffn_wo"], np.float32).reshape(2, 8, 64, 1024))
    w1 = np.asarray(inputs["L0_ffn_w1"], np.float32)
    HID = 2730
    w1a = w1[:, :, :HID]
    w1g = w1[:, :, HID:]
    pad = 22 * 128 - HID
    w1p = np.concatenate([
        np.pad(w1a, ((0, 0), (0, 0), (0, pad))),
        np.pad(w1g, ((0, 0), (0, 0), (0, pad)))], axis=2)
    w1p = np.ascontiguousarray(w1p)
    w2 = np.asarray(inputs["L0_ffn_w2"], np.float32)
    w2p = np.ascontiguousarray(np.pad(w2, ((0, 0), (0, pad), (0, 0))))
    b1 = np.asarray(inputs["L0_ffn_b1"], np.float32)
    b1p = np.stack([np.pad(b1[:, :HID], ((0, 0), (0, pad))),
                    np.pad(b1[:, HID:], ((0, 0), (0, pad)))], axis=1)
    b2 = np.asarray(inputs["L0_ffn_b2"], np.float32)
    ln_g = np.asarray(inputs["L0_ffn_ln_g"], np.float32)
    ln_b = np.asarray(inputs["L0_ffn_ln_b"], np.float32)
    b1p_t = np.ascontiguousarray(b1p.reshape(2, 2, 22, 128).transpose(3, 0, 1, 2))
    b2_t = np.ascontiguousarray(b2.reshape(2, 8, 128).transpose(2, 0, 1))
    lng_t = np.ascontiguousarray(ln_g.reshape(2, 8, 128).transpose(2, 0, 1))
    lnb_t = np.ascontiguousarray(ln_b.reshape(2, 8, 128).transpose(2, 0, 1))
    qsc = np.asarray(inputs["L0_ffn_q_scale"], np.float32)
    ksc = np.asarray(inputs["L0_ffn_k_scale"], np.float32)
    sw = np.asarray(inputs["L0_attn_scale_weights"], np.float32)

    ident_f = np.eye(128, dtype=np.float32)
    ident_b = np.eye(128, dtype=BF16NP)
    seld = np.zeros((128, 41, 128), dtype=BF16NP)
    for j in range(41):
        seld[j, j, :] = 1
        seld[64 + j, j, :] = 1
    sel64 = np.zeros((65, 128), dtype=BF16NP)
    sel64[64, :] = 1
    onesc = np.ones((128, 128), dtype=BF16NP)

    in_maps = []
    for c in range(g.n_cores):
        b, s = g.core_pos(c)
        xc = np.zeros((g.Tc, 1024), dtype=np.float32)
        lo = s - g.LH
        hi = s + g.own + g.RH
        src_lo, src_hi = max(lo, 0), min(hi, g.T_seq)
        xc[src_lo - lo: src_hi - lo] = x_full[b, src_lo:src_hi]
        tpos_lo, tpos_hi = lo, hi
        Aq, Bq, Ak, Bk = _rotary_tables(g.T_seq, tpos_lo, tpos_hi)
        for Aarr in (Aq, Bq, Ak, Bk):
            assert np.all(np.isfinite(Aarr))
        att_stripes = list(range(-2, g.n_own))
        mvar = np.zeros((len(att_stripes), 4, 128, 512), dtype=BF16NP)
        for i, k in enumerate(att_stripes):
            if s + k * g.W == 0:
                mvar[i] = np.full((4, 128, 512), NEG, dtype=BF16NP)
            else:
                mvar[i] = masks8[:4]
        in_maps.append({
            "x": xc,
            "Aq": Aq, "Bq": Bq, "Ak": Ak, "Bk": Bk,
            "mstat": np.ascontiguousarray(masks8[4:]),
            "mvar": mvar,
            "identf": ident_f, "identb": ident_b,
            "seld": seld, "sel64": sel64, "onesc": onesc,
            "WinT": W_inT, "WattnT": W_attnT, "WoutT": W_outT,
            "wqkv": wqkv, "woh": wo_h, "w1p": w1p, "w2p": w2p,
            "b1p": b1p_t, "b2": b2_t, "ln_g": lng_t, "ln_b": lnb_t,
            "qsc": qsc, "ksc": ksc, "sw": sw,
        })
    return in_maps



class Prog:
    EK = ("T", "V", "A", "G", "S")
    REAL_BARRIER = True

    def __init__(self, nc):
        self.nc = nc
        self.sems = {k: nc.alloc_semaphore(f"sem{k}") for k in self.EK}
        self.dsems = {"S": nc.alloc_semaphore("dmaS"), "G": nc.alloc_semaphore("dmaG")}
        self.q = {k: [] for k in self.EK}
        self.cnt = {k: 0 for k in self.EK}
        self.dcnt = {"S": 0, "G": 0}
        self.hw = {}

    def _dstream(self, key):
        if key not in self.dsems:
            self.dsems[key] = self.nc.alloc_semaphore(f"dma{key}")
            self.dcnt[key] = 0
        return self.dsems[key]

    def _wait(self, ek, dep):
        pk, val = dep
        key = (ek, pk)
        if self.hw.get(key, 0) >= val:
            return
        self.hw[key] = val
        sem = self.dsems[pk[1:]] if pk[0] == "d" else self.sems[pk]
        self.q[ek].append(lambda e, sem=sem, val=val: e.wait_ge(sem, val))

    def emit(self, ek, fn, deps=(), mark=False):
        for d in deps:
            if d is not None:
                self._wait(ek, d)
        if mark:
            self.cnt[ek] += 1
            sem = self.sems[ek]

            def thunk(e, fn=fn, sem=sem):
                fn(e)
                e.drain(fusable=True).then_inc(sem, 1)
            self.q[ek].append(thunk)
            return (ek, self.cnt[ek])
        self.q[ek].append(fn)
        return None

    def tok(self, ek):
        return (ek, self.cnt[ek])

    def vdrain(self):
        self.q["V"].append(lambda e: e.drain())

    def dma(self, qk, out_ap, in_ap, deps=(), stream=None):
        for d in deps:
            if d is not None:
                self._wait(qk, d)
        key = stream or qk
        dsem = self._dstream(key)
        self.dcnt[key] += 1
        self.q[qk].append(
            lambda e, o=out_ap, i=in_ap, dsem=dsem:
                e.dma_start(out=o, in_=i).then_inc(dsem, 16))
        return ("d" + key, 16 * self.dcnt[key])

    def barrier(self, with_dma=False):
        if with_dma:
            for k in self.EK:
                for key, cntv in self.dcnt.items():
                    self._wait(k, ("d" + key, 16 * cntv))
        self.finalize_block()

    def finalize_block(self):
        nc = self.nc
        if not any(self.q[k] for k in self.EK):
            return
        with nc.Block() as block:
            qs = {k: list(self.q[k]) for k in self.EK}

            def play(key):
                def f(e):
                    for t in qs[key]:
                        t(e)
                return f
            block.sync(play("S"))
            block.tensor(play("T"))
            block.vector(play("V"))
            block.scalar(play("A"))
            block.gpsimd(play("G"))
        for k in self.EK:
            self.q[k] = []



class Ctx:
    pass


def _carve(nc, zb, base, name, shape, dtype, cur):
    h = nc.alloc_sbuf_tensor_at(name, shape, dtype, offset=base + cur)
    per = 1
    for s in shape[1:]:
        per *= s
    per *= mybir.dt.size(dtype)
    per = (per + 63) // 64 * 64
    return h.ap(), cur + per


def build_program(geo):
    g = geo
    nc = bass.Bass("TRN2", target_bir_lowering=False,
                   detect_race_conditions=False)
    P = Prog(nc)
    c = Ctx()
    c.geo = g
    c.nc = nc

    def din(name, shape, dtype=F32):
        return nc.dram_tensor(name, list(shape), dtype, kind="ExternalInput").ap()

    c.x = din("x", (g.Tc, 1024))
    c.Aq = din("Aq", (g.Tc, 64)); c.Bq = din("Bq", (g.Tc, 64))
    c.Ak = din("Ak", (g.Tc, 64)); c.Bk = din("Bk", (g.Tc, 64))
    c.mstat = din("mstat", (4, 128, 512), BF16)
    n_att = g.n_own + 2
    c.mvar = din("mvar", (n_att, 4, 128, 512), BF16)
    c.identf = din("identf", (128, 128))
    c.identb = din("identb", (128, 128), BF16)
    c.seld = din("seld", (128, 41, 128), BF16)
    c.sel64 = din("sel64", (65, 128), BF16)
    c.onesc = din("onesc", (128, 128), BF16)
    c.WinT = din("WinT", (1024, 1024))
    c.WattnT = din("WattnT", (1024, 656))
    c.WoutT = din("WoutT", (1024, 1024))
    c.wqkv = din("wqkv", (2, 1024, 1536))
    c.woh = din("woh", (2, 8, 64, 1024))
    c.w1p = din("w1p", (2, 1024, 5632))
    c.w2p = din("w2p", (2, 2816, 1024))
    c.b1p = din("b1p", (128, 2, 2, 22))
    c.b2 = din("b2", (128, 2, 8))
    c.ln_g = din("ln_g", (128, 2, 8)); c.ln_b = din("ln_b", (128, 2, 8))
    c.qsc = din("qsc", (2, 64)); c.ksc = din("ksc", (2, 64))
    c.sw = din("sw", (2,))
    c.out = nc.dram_tensor("out", [g.own, 1024], F32, kind="ExternalOutput").ap()

    def dint(name, shape, dtype=BF16):
        return nc.dram_tensor(name, list(shape), dtype, kind="Internal").ap()

    c.hWinT = dint("hWinT", (8, 1024, 2, 128))
    c.hWattnT = dint("hWattnT", (1024, 2, 656))
    c.hWoutT = dint("hWoutT", (1024, 2, 1024))
    c.hwqkv = dint("hwqkv", (2, 1024, 2, 1536))
    c.hwoh = dint("hwoh", (2, 8, 64, 2, 1024))
    c.hw1p = dint("hw1p", (2, 44, 1024, 2, 128))
    c.hw2p = dint("hw2p", (2, 2816, 2, 1024))
    nslot = g.n_own + g.warm + 1
    c.kspill = dint("kspill", (2, nslot, 2, 512, 512))
    c.vspill = dint("vspill", (2, nslot, 2, 512, 520))

    c.PS_A = nc.alloc_psum_tensor("PS_A", [128, 2048], F32).ap()
    c.PS_B = nc.alloc_psum_tensor("PS_B", [128, 1024], F32).ap()
    c.PS_C = nc.alloc_psum_tensor("PS_C", [128, 512], F32).ap()
    c.PS_D = nc.alloc_psum_tensor("PS_D", [128, 512], F32).ap()

    A = nc.alloc_sbuf_tensor
    c.identf_s = A("identf_s", [128, 128], F32).ap()
    c.identb_s = A("identb_s", [128, 128], BF16).ap()
    c.seld_s = A("seld_s", [128, 41, 128], BF16).ap()
    c.sel64_s = A("sel64_s", [65, 128], BF16).ap()
    c.ones_s = A("ones_s", [128, 128], BF16).ap()
    c.mstat_s = A("mstat_s", [128, 4, 512], BF16).ap()
    c.mvar_s = A("mvar_s", [128, 4, 512], BF16).ap()
    c.tabs = A("tabs", [128, 4, 4, 64], F32).ap()
    c.tabe = A("tabe", [128, 2, 64], F32).ap()
    c.qs_s = A("qs_s", [128, 2, 2, 64], F32).ap()
    c.ks_s = A("ks_s", [128, 2, 2, 64], F32).ap()
    c.lng_s = A("lng_s", [128, 2, 8], F32).ap()
    c.lnb_s = A("lnb_s", [128, 2, 8], F32).ap()
    c.b2_s = A("b2_s", [128, 2, 8], F32).ap()
    c.b1_s = A("b1_s", [128, 2, 2, 22], F32).ap()
    c.sw_s = A("sw_s", [128, 2], F32).ap()
    c.eps_s = A("eps_s", [128, 1], F32).ap()
    c.x1col = A("x1col", [128, 2, 8], F32).ap()
    c.o1col = A("o1col", [128, 8], F32).ap()
    c.sums2 = A("sums2", [65, 1024], F32).ap()
    c.sums2b = A("sums2b", [65, 1024], BF16).ap()
    c.cols = A("cols", [128, 16], F32).ap()
    c.barsc = A("barsc", [128, 5], F32).ap()
    c.barscb = A("barscb", [128, 1], BF16).ap()
    P.barscb = c.barscb
    P.barsc = c.barsc

    c.xn_f = A("xn_f", [128, 8, 512], F32).ap()
    c.xn_hi = A("xn_hi", [128, 8, 768], BF16).ap()
    c.xn_lo = A("xn_lo", [128, 8, 768], BF16).ap()
    c.x1_f = A("x1_f", [128, 8, 513], F32).ap()
    c.o1_f = A("o1_f", [128, 8, 513], F32).ap()
    c.acc = A("acc", [128, 4, 1024], F32).ap()
    c.brow = A("brow", [1, 256], BF16).ap()
    c.qkvw = A("qkvw", [128, 2, 2, 512], BF16).ap()
    c.w1w = A("w1w", [128, 2, 8, 2, 128], BF16).ap()
    mwh = A("mw", [128, 2, 8, 2, 128], BF16)
    c.mw = mwh.ap()
    mwbase = nc.lookup_mloc(mwh).addr
    c.wow = nc.alloc_sbuf_tensor_at("wow", [64, 2, 2, 1024], BF16,
                                    offset=mwbase).ap()

    ZBYTES = 68064
    zb_h = A("ZB", [128, ZBYTES], mybir.dt.uint8)
    zbase = nc.lookup_mloc(zb_h).addr
    cur = 0
    def CV(name, shape, dtype):
        nonlocal cur
        ap, cur2 = _carve(nc, zb_h, zbase, name, shape, dtype, cur)
        cur = cur2
        return ap
    c.qscr = CV("qscr", [128, 512], F32)
    c.qscr2 = CV("qscr2", [128, 512], F32)
    c.qfm_hi = CV("qfm_hi", [128, 4, 512], BF16)
    c.qfm_lo = CV("qfm_lo", [128, 4, 512], BF16)
    c.kcur_hi = CV("kcur_hi", [128, 4, 512], BF16)
    c.kcur_lo = CV("kcur_lo", [128, 4, 512], BF16)
    c.vcur_hi = CV("vcur_hi", [128, 4, 520], BF16)
    c.vcur_lo = CV("vcur_lo", [128, 4, 520], BF16)
    c.kprv_hi = CV("kprv_hi", [128, 4, 512], BF16)
    c.kprv_lo = CV("kprv_lo", [128, 4, 512], BF16)
    c.vprv_hi = CV("vprv_hi", [128, 4, 520], BF16)
    c.vprv_lo = CV("vprv_lo", [128, 4, 520], BF16)
    c.P_hi = CV("P_hi", [128, 2048], BF16)
    c.P_lo = CV("P_lo", [128, 2048], BF16)
    c.P_f = CV("P_f", [128, 512], F32)
    c.o_hi = CV("o_hi", [64, 2, 512], BF16)
    c.o_lo = CV("o_lo", [64, 2, 512], BF16)
    c.oscr = CV("oscr", [64, 512], F32)
    c.recipb = CV("recipb", [128, 512], F32)
    attn_end = cur
    cur = 0
    c.zx = CV("zx", [128, 1024], F32)
    c.zxn = CV("zxn", [128, 1024], F32)
    c.a_f = CV("a_f", [128, 8, 768], F32)
    c.rj_hi = CV("rj_hi", [128, 2, 128], BF16)
    c.rj_lo = CV("rj_lo", [128, 2, 128], BF16)
    c.w_tm = CV("w_tm", [128, 4, 520], F32)
    c.wfm2 = CV("wfm2", [128, 4, 512], BF16)
    c.dtmp = CV("dtmp", [128, 1024], F32)
    c.dyn_f = CV("dyn_f", [128, 8, 512], F32)
    c.dj_hi = CV("dj_hi", [128, 2, 128], BF16)
    c.dj_lo = CV("dj_lo", [128, 2, 128], BF16)
    ph1_end = cur
    cur = 0
    c.h2_hi = CV("h2_hi", [128, 8, 512], BF16)
    c.h2_lo = CV("h2_lo", [128, 8, 512], BF16)
    c.h4_hi = CV("h4_hi", [128, 6, 512], BF16)
    c.h4_lo = CV("h4_lo", [128, 6, 512], BF16)
    W2B_OFF = cur
    c.w2b_hi = CV("w2b_hi", [128, 6, 1024], BF16)
    c.w2b_lo = CV("w2b_lo", [128, 6, 1024], BF16)
    c.gscr = CV("gscr", [128, 512], F32)
    c.gscr2 = CV("gscr2", [128, 512], F32)
    c.sqh = CV("sqh", [128, 512], BF16)
    c.sql = CV("sql", [128, 512], BF16)
    c.statr = CV("statr", [1, 2, 512], F32)
    c.statb = CV("statb", [1, 2, 1024], BF16)
    mlp_end = cur
    c.x1s_hi, _ = _carve(nc, zb_h, zbase, "x1s_hi", [128, 8, 513], BF16, W2B_OFF)
    c.x1s_lo, _ = _carve(nc, zb_h, zbase, "x1s_lo", [128, 8, 513], BF16, W2B_OFF + 8256)
    c.o1s_hi, _ = _carve(nc, zb_h, zbase, "o1s_hi", [128, 8, 513], BF16, W2B_OFF)
    c.o1s_lo, _ = _carve(nc, zb_h, zbase, "o1s_lo", [128, 8, 513], BF16, W2B_OFF + 8256)
    assert attn_end <= ZBYTES and ph1_end <= ZBYTES and mlp_end <= ZBYTES, (
        attn_end, ph1_end, mlp_end)
    c._zb = (zb_h, zbase)

    c.free = {}
    c.P = P
    return nc, P, c



def mm3(P, ps, lh, ll, rh, rl, start, stop, deps=(), mark=False):
    if globals().get("MM_HIONLY"):
        return P.emit("T", lambda e: e.matmul(ps, lh, rh, start=start, stop=stop),
                      deps=deps, mark=mark)
    P.emit("T", lambda e: e.matmul(ps, lh, rh, start=start, stop=False), deps=deps)
    P.emit("T", lambda e: e.matmul(ps, lh, rl, start=False, stop=False))
    return P.emit("T", lambda e: e.matmul(ps, ll, rh, start=False, stop=stop),
                  mark=mark)


def emit_startup(P, c, parts=("consts", "sw", "qsks", "split"), njobs=None):
    g = c.geo
    nc = c.nc
    tload = []
    tload.append(P.dma("S", c.identf_s, c.identf[:, :]))
    tload.append(P.dma("S", c.identb_s, c.identb[:, :]))
    tload.append(P.dma("S", c.seld_s, c.seld[:, :, :]))
    tload.append(P.dma("S", c.sel64_s, c.sel64[:, :]))
    tload.append(P.dma("S", c.ones_s, c.onesc[:, :]))
    tload.append(P.dma("S", c.mstat_s, c.mstat.rearrange("a p c -> p a c")))
    tload.append(P.dma("S", c.lng_s, c.ln_g[:, :, :]))
    tload.append(P.dma("S", c.lnb_s, c.ln_b[:, :, :]))
    tload.append(P.dma("S", c.b2_s, c.b2[:, :, :]))
    tload.append(P.dma("S", c.b1_s, c.b1p[:, :, :, :]))
    tld = tload[-1]
    P.emit("V", lambda e: e.memset(c.eps_s, 1e-5))
    P.emit("V", lambda e: e.memset(c.barsc[:, 3:4], -0.5))
    P.emit("V", lambda e: e.memset(c.barsc[:, 4:5], 1.5))
    P.emit("V", lambda e: e.memset(c.x1col, 0.0))
    P.emit("V", lambda e: e.memset(c.o1col, 0.0))
    P.emit("V", lambda e: e.memset(c.sums2, 0.0))
    P.emit("V", lambda e: e.memset(c.sums2b, 0.0))
    P.emit("V", lambda e: e.memset(c.cols, 0.0))
    tmem = P.emit("V", lambda e: e.memset(c.dtmp, 0.0), mark=True)

    tsw = tld
    tq = [tld]
    if "sw" in parts:
        t0 = P.dma("S", c.sums2[0:1, 0:2], c.sw.rearrange("(o a) -> o a", o=1),
                   deps=[tmem])
        def swchain(e):
            return e.reduce_max(c.sums2[0:1, 4:5], c.sums2[0:1, 0:2], axis=AX.X)
        P.emit("V", swchain, deps=[t0, tmem])
        P.vdrain()
        tv = P.emit("V", lambda e: e.tensor_scalar_mul(c.sums2[0:1, 5:6],
                                                       c.sums2[0:1, 4:5], -1.0),
                    mark=True)
        ta = P.emit("A", lambda e: e.activation(c.sums2[0:1, 2:4], c.sums2[0:1, 0:2],
                                                AF.Exp, bias=c.sums2[0:1, 5:6],
                                                accum_out=c.sums2[0:1, 6:7]),
                    deps=[tv], mark=True)
        P.emit("V", lambda e: e.reciprocal(c.sums2[0:1, 7:8], c.sums2[0:1, 6:7]), deps=[ta])
        P.vdrain()
        tv = P.emit("V", lambda e: e.tensor_scalar(c.sums2[0:1, 2:4], c.sums2[0:1, 2:4],
                                                   c.sums2[0:1, 7:8], None, op0=OP.mult),
                    mark=True)
        ta = P.emit("A", lambda e: e.copy(c.statb[0:1, 0, 0:2], c.sums2[0:1, 2:4]),
                    deps=[tv], mark=True)
        tv = P.emit("V", lambda e: e.tensor_tensor(c.statb[0:1, 0, 2:4], c.sums2[0:1, 2:4],
                                                   c.statb[0:1, 0, 0:2], op=OP.subtract),
                    deps=[ta], mark=True)
        tT = P.emit("T", lambda e: e.matmul(c.PS_C[:, 0:2], c.ones_s[0:1, 0:128],
                                            c.statb[0:1, 0, 0:2], start=True, stop=False),
                    deps=[tv, tld])
        tT = P.emit("T", lambda e: e.matmul(c.PS_C[:, 0:2], c.ones_s[0:1, 0:128],
                                            c.statb[0:1, 0, 2:4], start=False, stop=True),
                    mark=True)
        tsw = P.emit("A", lambda e: e.copy(c.sw_s, c.PS_C[:, 0:2]), deps=[tT], mark=True)
        tq = [tsw]

    if "qsks" in parts:
        for li in range(2):
            for (src, dst, name) in ((c.qsc, c.qs_s, "q"), (c.ksc, c.ks_s, "k")):
                t0 = P.dma("S", c.sums2[0:1, 0:64], src[li:li+1, :],
                           deps=[tmem, P.tok("V")])
                ta = P.emit("A", lambda e: e.copy(c.brow[0:1, 0:64], c.sums2[0:1, 0:64]),
                            deps=[t0, tsw], mark=True)
                tv = P.emit("V", lambda e: e.tensor_tensor(
                    c.brow[0:1, 64:128], c.sums2[0:1, 0:64], c.brow[0:1, 0:64],
                    op=OP.subtract), deps=[ta], mark=True)
                P.emit("T", lambda e: e.matmul(c.PS_C[:, 0:64], c.ones_s[0:1, 0:128],
                                               c.brow[0:1, 0:64], start=True, stop=False),
                       deps=[tv])
                tT = P.emit("T", lambda e: e.matmul(c.PS_C[:, 0:64], c.ones_s[0:1, 0:128],
                                                    c.brow[0:1, 64:128], start=False,
                                                    stop=True), mark=True)
                tq.append(P.emit("A", lambda e, dst=dst, li=li: e.copy(
                    dst[:, li, 0, :], c.PS_C[:, 0:64]), deps=[tT], mark=True))
                tq.append(P.emit("A", lambda e, dst=dst, li=li: e.copy(
                    dst[:, li, 1, 0:32], c.PS_C[:, 32:64]), mark=True))
                tq.append(P.emit("A", lambda e, dst=dst, li=li: e.copy(
                    dst[:, li, 1, 32:64], c.PS_C[:, 0:32]), mark=True))

    P.emit("V", lambda e: e.memset(c.sums2, 0.0), deps=[tq[-1], P.tok("A")])
    tz = P.emit("V", lambda e: e.memset(c.sums2b, 0.0), mark=True)

    ws_f, ws_hi, ws_lo = [], [], []
    zb = c._zb
    for b in range(2):
        f, _ = _carve(nc, zb[0], zb[1], f"ws_f{b}", [128, 4096], F32, b * 32768)
        h, _ = _carve(nc, zb[0], zb[1], f"ws_hi{b}", [128, 4096], BF16, b * 32768 + 16384)
        l, _ = _carve(nc, zb[0], zb[1], f"ws_lo{b}", [128, 4096], BF16, b * 32768 + 24576)
        ws_f.append(f); ws_hi.append(h); ws_lo.append(l)

    jobs = []
    def add_jobs(src, hdst, rtiles, C):
        for r in range(rtiles):
            jobs.append((src, hdst, r, C))

    add_jobs(lambda r: c.WinT[128*r:128*(r+1), :],
             lambda r, dt: c.hWinT[:, 128*r:128*(r+1), dt, :]
             .rearrange("ec p col -> p ec col"), 8, 1024)
    add_jobs(lambda r: c.WattnT[128*r:128*(r+1), :],
             lambda r, dt: c.hWattnT[128*r:128*(r+1), dt, :], 8, 656)
    add_jobs(lambda r: c.WoutT[128*r:128*(r+1), :],
             lambda r, dt: c.hWoutT[128*r:128*(r+1), dt, :], 8, 1024)
    for li in range(2):
        add_jobs(lambda r, li=li: c.wqkv[li, 128*r:128*(r+1), :],
                 lambda r, dt, li=li: c.hwqkv[li, 128*r:128*(r+1), dt, :], 8, 1536)
        add_jobs(lambda r, li=li: c.woh[li].rearrange("h s e -> (h s) e")[128*r:128*(r+1), :],
                 lambda r, dt, li=li: c.hwoh[li].rearrange("h s d e -> (h s) d e")[128*r:128*(r+1), dt, :],
                 4, 1024)
        for half in range(2):
            add_jobs(lambda r, li=li, half=half: c.w1p[li, 128*r:128*(r+1),
                                                       2816*half:2816*(half+1)],
                     lambda r, dt, li=li, half=half: c.hw1p[li, 22*half:22*half+22,
                                                            128*r:128*(r+1), dt, :]
                     .rearrange("ch p col -> p ch col"), 8, 2816)
        add_jobs(lambda r, li=li: c.w2p[li, 128*r:128*(r+1), :],
                 lambda r, dt, li=li: c.hw2p[li, 128*r:128*(r+1), dt, :], 22, 1024)

    if "split" not in parts:
        jobs = []
    elif njobs is not None:
        jobs = jobs[:njobs]
    free_tok = [tz, tz]
    st_tok = [None, None]
    for i, (src, hdst, r, C) in enumerate(jobs):
        b = i % 2
        tin = P.dma("S", ws_f[b][:, 0:C], src(r), deps=[free_tok[b]])
        ta = P.emit("A", lambda e, b=b, C=C: e.copy(ws_hi[b][:, 0:C], ws_f[b][:, 0:C]),
                    deps=[tin, st_tok[b]], mark=True)
        tv = P.emit("V", lambda e, b=b, C=C: e.tensor_tensor(
            ws_lo[b][:, 0:C], ws_f[b][:, 0:C], ws_hi[b][:, 0:C], op=OP.subtract),
            deps=[ta], mark=True)
        t1 = P.dma("G", hdst(r, 0), ws_hi[b][:, 0:C], deps=[ta])
        t2 = P.dma("G", hdst(r, 1), ws_lo[b][:, 0:C], deps=[tv])
        free_tok[b] = tv
        st_tok[b] = t2
    P.barrier(with_dma=True)


def _v3(ps_slice, a=8, b=128):
    return ps_slice.rearrange("p (a b) -> p a b", a=a)


def emit_ph0(P, c, k):
    g = c.geo
    toff = g.t0_off(k)
    ntiles = 6 if k >= 0 else 5
    if globals().get("PH0_TILES"):
        ntiles = min(ntiles, globals()["PH0_TILES"])
    inv = 1.0 / 1024.0
    zx_free = c.free.get("zx")
    psA_free = c.free.get("PS_A")
    tT_prev = None
    for j in range(ntiles):
        trel = -128 + 128 * j
        rows = toff + trel
        tin = P.dma("S", c.zx, c.x[rows:rows + 128, :], deps=[zx_free], stream="x")
        tv1 = P.emit("V", lambda e: e.reduce_sum(c.cols[:, 0:1], c.zx, axis=AX.X),
                     deps=[tin], mark=True)
        P.emit("V", lambda e: e.tensor_tensor(c.zxn, c.zx, c.zx, op=OP.mult),
               deps=[tin, tT_prev])
        P.emit("V", lambda e: e.reduce_sum(c.cols[:, 1:2], c.zxn, axis=AX.X))
        P.emit("V", lambda e: e.tensor_scalar_mul(c.cols[:, 2:3], c.cols[:, 0:1], inv))
        P.emit("V", lambda e: e.tensor_scalar_mul(c.cols[:, 3:4], c.cols[:, 1:2], inv))
        P.emit("V", lambda e: e.tensor_tensor(c.cols[:, 4:5], c.cols[:, 2:3],
                                              c.cols[:, 2:3], op=OP.mult))
        P.vdrain()
        tv2 = P.emit("V", lambda e: e.tensor_tensor(c.cols[:, 3:4], c.cols[:, 3:4],
                                                    c.cols[:, 4:5], op=OP.subtract),
                     mark=True)
        ta2 = P.emit("A", lambda e: e.activation(c.cols[:, 5:6], c.cols[:, 3:4],
                                                 AF.Sqrt, bias=c.eps_s),
                     deps=[tv2], mark=True)
        P.emit("V", lambda e: e.reciprocal(c.cols[:, 6:7], c.cols[:, 5:6]), deps=[ta2])
        P.emit("V", lambda e: e.tensor_scalar(c.cols[:, 7:8], c.cols[:, 3:4],
                                              c.eps_s, None, op0=OP.add))
        P.emit("V", lambda e: e.tensor_tensor(c.cols[:, 8:9], c.cols[:, 6:7],
                                              c.cols[:, 6:7], op=OP.mult))
        P.vdrain()
        P.emit("V", lambda e: e.tensor_tensor(c.cols[:, 9:10], c.cols[:, 7:8],
                                              c.cols[:, 8:9], op=OP.mult))
        P.vdrain()
        P.emit("V", lambda e: e.tensor_scalar(c.cols[:, 10:11], c.cols[:, 9:10],
                                              c.barsc[:, 3:4], c.barsc[:, 4:5],
                                              op0=OP.mult, op1=OP.add))
        P.vdrain()
        P.emit("V", lambda e: e.tensor_tensor(c.cols[:, 6:7], c.cols[:, 6:7],
                                              c.cols[:, 10:11], op=OP.mult))
        P.vdrain()
        tv4 = P.emit("V", lambda e: e.tensor_scalar(c.zxn, c.zx, c.cols[:, 2:3],
                                                    c.cols[:, 6:7], op0=OP.subtract,
                                                    op1=OP.mult),
                     mark=True)
        for d in range(8):
            P.emit("T", lambda e, d=d: e.transpose(c.PS_A[:, 128 * d:128 * d + 128],
                                                   c.zxn[:, 128 * d:128 * d + 128],
                                                   c.identf_s),
                   deps=[tv4, psA_free] if d == 0 else ())
        tT = P.emit("T", lambda e: e.nop(), mark=True)
        tT_prev = tT
        col = 128 * j
        psv = _v3(c.PS_A[:, 0:1024])
        ta = P.emit("A", lambda e, col=col: e.activation(
            c.xn_hi[:, :, col:col + 128], psv, AF.Copy), deps=[tT], mark=True)
        scr = _v3(c.dtmp[:, 0:1024])
        P.emit("V", lambda e, col=col, scr=scr: e.tensor_tensor(
            scr, psv, c.xn_hi[:, :, col:col + 128], op=OP.subtract),
            deps=[ta])
        tv = P.emit("V", lambda e, col=col, scr=scr: e.tensor_copy(
            c.xn_lo[:, :, col:col + 128], scr), mark=True)
        tf = None
        if 0 <= trel < 512 and not globals().get("NO_XNF"):
            tf = P.emit("V", lambda e, trel=trel: e.tensor_copy(
                c.xn_f[:, :, trel:trel + 128], psv), mark=True)
        psA_free = tv if tf is None else tf
        zx_free = tv4
    c.free["zx"] = zx_free
    c.free["PS_A"] = psA_free
    c.free["xnA"] = P.tok("A")
    c.free["xnV"] = P.tok("V")


def emit_ph1(P, c, k):
    g = c.geo
    toff = g.t0_off(k)
    tacc = P.dma("S", c.acc,
                 c.x[toff:toff + 512, :].rearrange("(a p) e -> p a e", p=128),
                 deps=[c.free.get("acc")], stream="acc")
    psA_free = c.free.get("PS_A")
    psB_free = c.free.get("PS_B")
    psC_free = c.free.get("PS_C")
    mwb = 0
    mw_free = [c.free.get("mw0"), c.free.get("mw1")]
    a_ev = None
    for e8 in range(8):
        tw = P.dma("S", c.mw[:, mwb].rearrange("p dc dt col -> p dc (dt col)"),
                   c.hWinT[e8].rearrange("(dc p) dt col -> p dc (dt col)", p=128),
                   deps=[mw_free[mwb]], stream=f"mwa{mwb}")
        mwv = c.mw[:, mwb]
        if globals().get("A_ILV"):
            for d in range(8):
                lh = mwv[:, d, 0, :]
                ll = mwv[:, d, 1, :]
                first = (d == 0)
                last = (d == 7)
                dep0 = [tw, psA_free, a_ev, c.free.get("xnA"), c.free.get("xnV")] if first else ()
                mm3(P, c.PS_A[:, 0:512], lh, ll, c.xn_hi[:, d, 0:512],
                    c.xn_lo[:, d, 0:512], first, last, deps=dep0)
                mm3(P, c.PS_A[:, 512:768], lh, ll, c.xn_hi[:, d, 512:768],
                    c.xn_lo[:, d, 512:768], first, last)
        else:
            for half, (h0, h1) in enumerate(((0, 512), (512, 768))):
                for d in range(8):
                    lh = mwv[:, d, 0, :]
                    ll = mwv[:, d, 1, :]
                    first = (d == 0)
                    last = (d == 7)
                    dep0 = ([tw, psA_free, a_ev, c.free.get("xnA"),
                             c.free.get("xnV")] if first and half == 0 else ())
                    mm3(P, c.PS_A[:, h0:h1], lh, ll, c.xn_hi[:, d, h0:h1],
                        c.xn_lo[:, d, h0:h1], first, last, deps=dep0)
        tT = P.emit("T", lambda e: e.nop(), mark=True)
        mw_free[mwb] = tT
        mwb ^= 1
        a_ev = P.emit("A", lambda e, e8=e8: e.activation(
            c.a_f[:, e8, :], c.PS_A[:, 0:768], AF.Copy), deps=[tT], mark=True)
    psA_free = a_ev

    P.emit("V", lambda e: e.memset(c.w_tm, 0.0))
    rj_free = [None, None]
    w_done = []
    for t in range(4):
        pm_last = None
        for e8 in range(8):
            b = (t * 8 + e8) % 2
            mwfl = c.mw[:, mwb].rearrange("p a b2 c2 -> p (a b2 c2)")
            tw = P.dma("S", mwfl[:, 0:1312].rearrange("p (dt col) -> p dt col", dt=2),
                       c.hWattnT[128 * e8:128 * e8 + 128, :, :],
                       deps=[mw_free[mwb]], stream=f"mwp{mwb}")
            asl = c.a_f[:, e8, 128 + 128 * t: 256 + 128 * t]
            ta = P.emit("A", lambda e, asl=asl, b=b: e.activation(
                c.rj_hi[:, b, :], asl, AF.Relu), deps=[a_ev, rj_free[b]], mark=True)
            tv = P.emit("V", lambda e, asl=asl, b=b: e.scalar_tensor_tensor(
                c.rj_lo[:, b, :], asl, 0.0, c.rj_hi[:, b, :],
                op0=OP.max, op1=OP.subtract), deps=[ta], mark=True)
            first = (e8 == 0)
            last = (e8 == 7)
            dep0 = [tw, tv, psB_free] if first else [tw, tv]
            mwv = mwfl[:, 0:1312].rearrange("p (dt col) -> p dt col", dt=2)
            mm3(P, c.PS_B[:, 0:512], c.rj_hi[:, b, :], c.rj_lo[:, b, :],
                mwv[:, 0, 0:512], mwv[:, 1, 0:512], first, last, deps=dep0)
            mm3(P, c.PS_B[:, 512:656], c.rj_hi[:, b, :], c.rj_lo[:, b, :],
                mwv[:, 0, 512:656], mwv[:, 1, 512:656], first, last)
            pm_last = P.emit("T", lambda e: e.nop(), mark=True)
            mw_free[mwb] = pm_last
            rj_free[b] = pm_last
            mwb ^= 1
        segs = [(0, 0, 21, 0), (328, 192, 41, 1)]
        tv_end = None
        for (p0, c0, cs, si) in segs:
            pv = c.PS_B[:, p0:p0 + 8 * cs].rearrange("p (h c) -> p h c", h=8)
            tm = P.emit("V", lambda e, pv=pv: e.reduce_max(
                c.cols[:, 0:8], pv, axis=AX.X), deps=[pm_last], mark=True)
            tn = P.emit("V", lambda e: e.tensor_scalar_mul(
                c.cols[:, 8:16], c.cols[:, 0:8], -1.0), mark=True)
            exps = []
            for h in range(8):
                exps.append(P.emit("A", lambda e, pv=pv, h=h, c0=c0, cs=cs, t=t:
                                   e.activation(c.w_tm[:, t, c0 + h * cs: c0 + (h + 1) * cs],
                                                pv[:, h, :], AF.Exp,
                                                bias=c.cols[:, 8 + h:9 + h]),
                                   deps=[tn], mark=True))
            wv = c.w_tm[:, t, c0:c0 + 8 * cs].rearrange("p (h c) -> p h c", h=8)
            ts_ = P.emit("V", lambda e, wv=wv: e.reduce_sum(
                c.cols[:, 0:8], wv, axis=AX.X), deps=[exps[-1]], mark=True)
            P.emit("V", lambda e: e.reciprocal(c.cols[:, 8:16], c.cols[:, 0:8]))
            P.vdrain()
            P.emit("V", lambda e, si=si: e.tensor_scalar(
                c.cols[:, 8:16], c.cols[:, 8:16], c.sw_s[:, si:si + 1], None,
                op0=OP.mult))
            P.vdrain()
            tv_end = P.emit("V", lambda e, wv=wv, cs=cs: e.tensor_tensor(
                wv, wv, c.cols[:, 8:16].rearrange("p (h o) -> p h o", o=1)
                .broadcast_to([128, 8, cs]), op=OP.mult), mark=True)
        w_done.append(tv_end)
        psB_free = tv_end
    if globals().get("PH1_TRUNC"):
        c.free["PS_A"] = psA_free
        c.free["PS_B"] = psB_free
        c.free["mw0"] = mw_free[0]
        c.free["mw1"] = mw_free[1]
        c.free["acc_ev"] = tacc
        return

    dyn_last = None
    tr_prev = None
    for si, (c0, cs, pad) in enumerate([(0, 21, 10), (192, 41, 20)]):
        for gi in range(2):
            tms = P.emit("V", lambda e: e.memset(c.wfm2, 0.0),
                         deps=[dyn_last, P.tok("T")], mark=True)
            ev_done = [tms]
            for h4 in range(4):
                h = 4 * gi + h4
                for t in range(4):
                    seg = c.w_tm[:, t, c0 + h * cs: c0 + h * cs + 41]
                    segd = seg.rearrange("p (o c) -> p o c", o=1).broadcast_to([128, 2, 41])
                    dup = c.dtmp[:, 0:128]
                    dupv = dup.rearrange("p (o cc) -> p o cc", o=2)[:, :, 0:41]
                    tdup = P.emit("V", lambda e, segd=segd, dupv=dupv: e.tensor_copy(
                        dupv, segd), deps=[w_done[t], dyn_last, tr_prev], mark=True)
                    tr_prev = P.emit("T", lambda e, dup=dup, t=t: e.transpose(
                        c.PS_C[:, 128 * t:128 * t + 128], dup, c.identf_s),
                        deps=[tdup, psC_free], mark=True)
                tT = P.emit("T", lambda e: e.nop(), mark=True)
                t1 = P.emit("A", lambda e, h4=h4: e.activation(
                    c.wfm2[0:41, h4, :], c.PS_C[0:41, :], AF.Copy), deps=[tT], mark=True)
                t2 = P.emit("A", lambda e, h4=h4: e.activation(
                    c.wfm2[64:105, h4, :], c.PS_C[64:105, :], AF.Copy), mark=True)
                P.emit("V", lambda e, h4=h4: e.tensor_tensor(
                    c.dtmp[64:105, 512:1024], c.PS_C[64:105, :],
                    c.wfm2[64:105, h4, :], op=OP.subtract), deps=[t2])
                t3 = P.emit("V", lambda e, h4=h4: e.tensor_copy(
                    c.wfm2[64:105, h4, :], c.dtmp[64:105, 512:1024]), mark=True)
                psC_free = t3
                ev_done += [t1, t3]
            wflat = c.wfm2.rearrange("p a b -> p (a b)")
            mulps = [None, None]
            for j in range(cs):
                dlt = j - pad
                for half in range(2):
                    tb = P.emit("T", lambda e, j=j, half=half: (
                        e.matmul(c.PS_A[:, 1024 * half: 1024 * half + 512],
                                 c.seld_s[:, j, :], wflat[:, 1024 * half: 1024 * half + 512],
                                 start=True, stop=True)),
                        deps=(ev_done + [psA_free]) if (j == 0 and half == 0) else [mulps[half]])
                    tb = P.emit("T", lambda e, j=j, half=half: (
                        e.matmul(c.PS_A[:, 1024 * half + 512: 1024 * half + 1024],
                                 c.seld_s[:, j, :], wflat[:, 1024 * half + 512: 1024 * half + 1024],
                                 start=True, stop=True)), mark=True)
                    ach = slice(4 * gi + 2 * half, 4 * gi + 2 * half + 2)
                    av = c.a_f[:, ach, 128 + dlt: 640 + dlt]
                    psv = c.PS_A[:, 1024 * half: 1024 * half + 1024].rearrange(
                        "p (a b) -> p a b", a=2)
                    tmpv = c.dtmp.rearrange("p (a b) -> p a b", a=2)
                    tm = P.emit("V", lambda e, av=av, psv=psv, tmpv=tmpv: e.tensor_tensor(
                        tmpv, av, psv, op=OP.mult), deps=[tb], mark=True)
                    mulps[half] = tm
                    dv = c.dyn_f[:, ach, :]
                    if si == 0 and j == 0:
                        dyn_last = P.emit("V", lambda e, dv=dv, tmpv=tmpv: e.tensor_copy(
                            dv, tmpv), mark=True)
                    else:
                        dyn_last = P.emit("V", lambda e, dv=dv, tmpv=tmpv: e.tensor_tensor(
                            dv, dv, tmpv, op=OP.add), mark=True)
    psA_free = dyn_last

    dj_free = [None, None]
    for t in range(4):
        for d in range(8):
            b = (t * 8 + d) % 2
            mwfl = c.mw[:, mwb].rearrange("p a b2 c2 -> p (a b2 c2)")
            tw = P.dma("S", mwfl.rearrange("p (dt col) -> p dt col", dt=2),
                       c.hWoutT[128 * d:128 * d + 128, :, :],
                       deps=[mw_free[mwb]], stream=f"mwo{mwb}")
            dsl = c.dyn_f[:, d, 128 * t:128 * t + 128]
            ta = P.emit("A", lambda e, dsl=dsl, b=b: e.activation(
                c.dj_hi[:, b, :], dsl, AF.Copy), deps=[dyn_last, dj_free[b]], mark=True)
            tv = P.emit("V", lambda e, dsl=dsl, b=b: e.tensor_tensor(
                c.dj_lo[:, b, :], dsl, c.dj_hi[:, b, :], op=OP.subtract),
                deps=[ta], mark=True)
            first = (d == 0)
            last = (d == 7)
            mwv = mwfl.rearrange("p (dt col) -> p dt col", dt=2)
            dep0 = [tw, tv, psB_free] if first else [tw, tv]
            mm3(P, c.PS_B[:, 0:512], c.dj_hi[:, b, :], c.dj_lo[:, b, :],
                mwv[:, 0, 0:512], mwv[:, 1, 0:512], first, last, deps=dep0)
            mm3(P, c.PS_B[:, 512:1024], c.dj_hi[:, b, :], c.dj_lo[:, b, :],
                mwv[:, 0, 512:1024], mwv[:, 1, 512:1024], first, last)
            tT = P.emit("T", lambda e: e.nop(), mark=True)
            mw_free[mwb] = tT
            dj_free[b] = tT
            mwb ^= 1
        psB_free = P.emit("V", lambda e, t=t: e.tensor_tensor(
            c.acc[:, t, :], c.acc[:, t, :], c.PS_B, op=OP.add),
            deps=[P.tok("T"), tacc], mark=True)
    c.free["PS_A"] = psA_free
    c.free["PS_B"] = psB_free
    c.free["PS_C"] = psC_free
    c.free["mw0"] = mw_free[0]
    c.free["mw1"] = mw_free[1]
    c.free["acc_ev"] = psB_free


def emit_ffn(P, c, k, l, mode):
    g = c.geo
    toff = g.t0_off(k)
    slot = k + g.warm
    full = (mode in ("full", "x1"))

    rdep = [P.tok("A"), P.tok("T")]
    if l == 0:
        P.emit("V", lambda e: e.tensor_copy(
            c.o1_f[:, :, 0:1].rearrange("p a o -> p (a o)"), c.o1col), deps=rdep)
        if full:
            P.emit("V", lambda e: e.tensor_copy(
                c.x1_f[:, :, 0:1].rearrange("p a o -> p (a o)"), c.x1col[:, 0, :]),
                deps=rdep)
    else:
        if full:
            P.emit("V", lambda e: e.tensor_copy(
                c.x1_f[:, :, 0:1].rearrange("p a o -> p (a o)"), c.x1col[:, 1, :]),
                deps=rdep)

    if l == 0:
        def IN(dt, d, t, sh):
            buf = c.xn_hi if dt == 0 else c.xn_lo
            c0 = 128 + 128 * t - (1 if sh else 0)
            return buf[:, d, c0:c0 + 128]
        in_ready = [c.free.get("xnA"), c.free.get("xnV")]
    else:
        ta = P.emit("A", lambda e: e.activation(c.o1s_hi, c.o1_f, AF.Copy),
                    deps=[c.free.get("o1_done"), P.tok("T")], mark=True)
        tv = P.emit("V", lambda e: e.tensor_tensor(c.o1s_lo, c.o1_f, c.o1s_hi,
                                                   op=OP.subtract), deps=[ta], mark=True)
        def IN(dt, d, t, sh):
            buf = c.o1s_hi if dt == 0 else c.o1s_lo
            c0 = 128 * t + (0 if sh else 1)
            return buf[:, d, c0:c0 + 128]
        in_ready = [ta, tv]

    qb = 0
    qw_free = [c.free.get("qkvw0"), c.free.get("qkvw1")]
    ev_prev = []
    psA_free = c.free.get("PS_A")
    psB_free = c.free.get("PS_B")
    kcur_sp = [None, None]
    vcur_sp = [None, None]
    for gi, grp in enumerate(("q", "k", "v")):
        if grp == "q" and not full:
            continue
        for d in range(8):
            tw = P.dma("S", c.qkvw[:, qb],
                       c.hwqkv[l, 128 * d:128 * d + 128, :, 512 * gi:512 * gi + 512],
                       deps=[qw_free[qb]], stream=f"qkv{qb}")
            for t in range(4):
                first = (d == 0)
                last = (d == 7)
                dep0 = ([tw, psA_free] + ev_prev + in_ready) if first else [tw]
                mm3(P, c.PS_A[:, 512 * t:512 * t + 512],
                    IN(0, d, t, d >= 4), IN(1, d, t, d >= 4),
                    c.qkvw[:, qb, 0, :], c.qkvw[:, qb, 1, :],
                    first, last, deps=dep0)
            tT = P.emit("T", lambda e: e.nop(), mark=True)
            qw_free[qb] = tT
            qb ^= 1
        tql = P.tok("T")
        ev_prev = []
        if grp in ("q", "k"):
            isq = (grp == "q")
            TA, TB = (0, 1) if isq else (2, 3)
            scv = c.qs_s if isq else c.ks_s
            dsth = c.qfm_hi if isq else c.kcur_hi
            dstl = c.qfm_lo if isq else c.kcur_lo
            dst_dep = [] if isq else [c.free.get(f"kcur{l}")]
            tT = None
            for t in range(4):
                pst = c.PS_A[:, 512 * t:512 * t + 512]
                ta0 = P.emit("A", lambda e, pst=pst: e.activation(
                    c.qscr, pst, AF.Copy), deps=[tql, tT], mark=True)
                ev_prev.append(ta0)
                P.emit("V", lambda e: e.tensor_tensor(c.qscr2, c.qscr, c.qscr,
                                                      op=OP.mult),
                       deps=[ta0], mark=False)
                tv1 = P.emit("V", lambda e: e.tensor_reduce(
                    c.cols[:, 0:8], _v3(c.qscr2, 8, 64), axis=AX.X, op=OP.add),
                    mark=True)
                tvg = P.emit("V", lambda e: e.tensor_scalar_max(
                    c.cols[:, 0:8], c.cols[:, 0:8], 1e-24), mark=True)
                ta2 = P.emit("A", lambda e: e.activation(
                    c.cols[:, 8:16], c.cols[:, 0:8], AF.Sqrt), deps=[tvg],
                    mark=True)
                P.emit("V", lambda e: e.reciprocal(c.cols[:, 8:16], c.cols[:, 8:16]),
                       deps=[ta2])
                P.vdrain()
                P.emit("V", lambda e: e.tensor_tensor(
                    c.qscr2[:, 0:8], c.cols[:, 8:16], c.cols[:, 8:16], op=OP.mult))
                P.vdrain()
                P.emit("V", lambda e: e.tensor_tensor(
                    c.qscr2[:, 0:8], c.cols[:, 0:8], c.qscr2[:, 0:8], op=OP.mult))
                P.vdrain()
                P.emit("V", lambda e: e.tensor_scalar(
                    c.qscr2[:, 0:8], c.qscr2[:, 0:8], c.barsc[:, 3:4],
                    c.barsc[:, 4:5], op0=OP.mult, op1=OP.add))
                P.vdrain()
                P.emit("V", lambda e: e.tensor_tensor(
                    c.cols[:, 8:16], c.cols[:, 8:16], c.qscr2[:, 0:8], op=OP.mult))
                P.vdrain()
                tv4 = P.emit("V", lambda e: e.tensor_tensor(
                    _v3(c.qscr2, 8, 64), _v3(c.qscr, 8, 64),
                    c.cols[:, 8:16].rearrange("p (h o) -> p h o", o=1)
                    .broadcast_to([128, 8, 64]), op=OP.mult),
                    deps=[ta0], mark=True)
                P.emit("V", lambda e, t=t, TA=TA, scv=scv: e.tensor_tensor(
                    c.tabe[:, 0, :], c.tabs[:, t, TA, :], scv[:, l, 0, :],
                    op=OP.mult), deps=[c.free.get("tabs_tok")])
                tv6 = P.emit("V", lambda e, t=t, TB=TB, scv=scv: e.tensor_tensor(
                    c.tabe[:, 1, :], c.tabs[:, t, TB, :], scv[:, l, 1, :],
                    op=OP.mult), mark=True)
                qv = _v3(c.qscr, 8, 64)
                qnv = _v3(c.qscr2, 8, 64)
                t0v = c.tabe[:, 0:1, :].broadcast_to([128, 8, 64])
                P.emit("V", lambda e, qv=qv, qnv=qnv, t0v=t0v: e.tensor_tensor(
                    qv, qnv, t0v, op=OP.mult), deps=[tv4, tv6])
                dt1 = _v3(c.dtmp[:, 0:256], 8, 32)
                dt2 = _v3(c.dtmp[:, 256:512], 8, 32)
                b1v = c.tabe[:, 1:2, 0:32].broadcast_to([128, 8, 32])
                b2v = c.tabe[:, 1:2, 32:64].broadcast_to([128, 8, 32])
                P.emit("V", lambda e, qnv=qnv, dt1=dt1, b1v=b1v: e.tensor_tensor(
                    dt1, qnv[:, :, 32:64], b1v, op=OP.mult))
                P.emit("V", lambda e, qv=qv, dt1=dt1: e.tensor_tensor(
                    qv[:, :, 0:32], qv[:, :, 0:32], dt1, op=OP.add))
                P.emit("V", lambda e, qnv=qnv, dt2=dt2, b2v=b2v: e.tensor_tensor(
                    dt2, qnv[:, :, 0:32], b2v, op=OP.mult))
                tv7 = P.emit("V", lambda e, qv=qv, dt2=dt2: e.tensor_tensor(
                    qv[:, :, 32:64], qv[:, :, 32:64], dt2, op=OP.add), mark=True)
                for c4 in range(4):
                    P.emit("T", lambda e, c4=c4: e.transpose(
                        c.PS_B[:, 128 * c4:128 * c4 + 128],
                        c.qscr[:, 128 * c4:128 * c4 + 128], c.identf_s),
                        deps=[tv7, psB_free] if c4 == 0 else ())
                tT = P.emit("T", lambda e: e.nop(), mark=True)
                psv = _v3(c.PS_B[:, 0:512], 4, 128)
                ta = P.emit("A", lambda e, t=t, dsth=dsth, psv=psv: e.activation(
                    dsth[:, :, 128 * t:128 * t + 128], psv, AF.Copy),
                    deps=[tT] + dst_dep, mark=True)
                scr = _v3(c.dtmp[:, 512:1024], 4, 128)
                P.emit("V", lambda e, t=t, dsth=dsth, psv=psv, scr=scr:
                       e.tensor_tensor(scr, psv,
                                       dsth[:, :, 128 * t:128 * t + 128],
                                       op=OP.subtract), deps=[ta])
                tv = P.emit("V", lambda e, t=t, dstl=dstl, scr=scr: e.tensor_copy(
                    dstl[:, :, 128 * t:128 * t + 128], scr), mark=True)
                psB_free = tv
            if grp == "k":
                kcur_sp[0] = P.dma("G", c.kspill[l, slot, 0].rearrange(
                    "(cc p) t -> p cc t", p=128), c.kcur_hi, deps=[P.tok("A")],
                    stream="ksp")
                kcur_sp[1] = P.dma("G", c.kspill[l, slot, 1].rearrange(
                    "(cc p) t -> p cc t", p=128), c.kcur_lo, deps=[P.tok("V")],
                    stream="ksp")
        else:
            vones_h = c.vcur_hi.rearrange("p t (h u) -> p t h u", u=65)[:, :, :, 64]
            vones_l = c.vcur_lo.rearrange("p t (h u) -> p t h u", u=65)[:, :, :, 64]
            tm1 = P.emit("V", lambda e: e.memset(vones_h, 1.0),
                         deps=[c.free.get(f"vcur{l}")], mark=True)
            tm2 = P.emit("V", lambda e: e.memset(vones_l, 0.0), mark=True)
            va_h = c.vcur_hi.rearrange("p t (h u) -> p t h u", u=65)[:, :, :, 0:64]
            va_l = c.vcur_lo.rearrange("p t (h u) -> p t h u", u=65)[:, :, :, 0:64]
            for t in range(4):
                psv = _v3(c.PS_A[:, 512 * t:512 * t + 512], 8, 64)
                ta = P.emit("A", lambda e, t=t, psv=psv: e.activation(
                    va_h[:, t], psv, AF.Copy), deps=[tql, tm1], mark=True)
                ev_prev.append(ta)
                scr = _v3(c.dtmp[:, 512:1024], 8, 64)
                P.emit("V", lambda e, t=t, psv=psv, scr=scr: e.tensor_tensor(
                    scr, psv, va_h[:, t], op=OP.subtract), deps=[ta, tm2])
                tv = P.emit("V", lambda e, t=t, scr=scr: e.tensor_copy(
                    va_l[:, t], scr), mark=True)
            vcur_sp[0] = P.dma("G", c.vspill[l, slot, 0].rearrange(
                "(cc p) t -> p cc t", p=128), c.vcur_hi, deps=[P.tok("A")],
                stream="vsp")
            vcur_sp[1] = P.dma("G", c.vspill[l, slot, 1].rearrange(
                "(cc p) t -> p cc t", p=128), c.vcur_lo, deps=[P.tok("V")],
                stream="vsp")
    c.free[f"kcur{l}"] = kcur_sp[1]
    c.free[f"vcur{l}"] = vcur_sp[1]
    c.free[f"kspill_w{l}"] = [t for t in kcur_sp + vcur_sp if t is not None]
    c.free["PS_A"] = ev_prev[-1] if ev_prev else psA_free
    c.free["PS_B"] = psB_free
    c.free["qkvw0"] = qw_free[0]
    c.free["qkvw1"] = qw_free[1]
    if not full:
        return

    tkp = [P.dma("S", c.kprv_hi, c.kspill[l, slot - 1, 0].rearrange(
               "(cc p) t -> p cc t", p=128),
               deps=[c.free.get(f"kspill_r{l}"), P.tok("V"), P.tok("A")]
               + c.free.get(f"kspill_w{l}", []), stream="kvl"),
           P.dma("S", c.kprv_lo, c.kspill[l, slot - 1, 1].rearrange(
               "(cc p) t -> p cc t", p=128), stream="kvl"),
           P.dma("S", c.vprv_hi, c.vspill[l, slot - 1, 0].rearrange(
               "(cc p) t -> p cc t", p=128), stream="kvl2"),
           P.dma("S", c.vprv_lo, c.vspill[l, slot - 1, 1].rearrange(
               "(cc p) t -> p cc t", p=128), stream="kvl2")]
    tmv = P.dma("S", c.mvar_s, c.mvar[k + 2].rearrange("a p t -> p a t"), stream="mvar",
                deps=[c.free.get("mvar")])
    qkv_done_A = P.tok("A")
    qkv_done_V = P.tok("V")
    psA_free = c.free.get("PS_A")
    psB_free = c.free.get("PS_B")
    ops_free = [c.free.get("PS_C"), c.free.get("PS_D")]
    wow_free = [c.free.get("wow0"), c.free.get("wow1")]
    o_free = [None, None]
    wo_ps_free = None
    exp_done = None
    x1_last = None
    for h in range(8):
        hb = h % 2
        opsum = c.PS_C if hb == 0 else c.PS_D
        base = 64 * (h % 2)
        c4h = h // 2
        twl = P.dma("S", c.wow[:, hb, :, :], c.hwoh[l, h], stream=f"wow{hb}",
                    deps=[wow_free[hb], c.free.get("mw0"), c.free.get("mw1")])
        tpv = None
        pv_prev = c.free.get("pv_prev")
        for half in range(2):
            for ci in range(4):
                cg = 4 * half + ci
                kbh = c.kprv_hi if cg < 4 else c.kcur_hi
                kbl = c.kprv_lo if cg < 4 else c.kcur_lo
                tb = cg % 4
                bank = c.PS_A[:, 512 * ci:512 * ci + 512]
                mask = c.mvar_s[:, ci, :] if cg < 4 else c.mstat_s[:, ci, :]
                dep0 = ([tmv, tkp[-1], exp_done, psA_free, qkv_done_A, qkv_done_V]
                        if (ci == 0) else ())
                P.emit("T", lambda e, bank=bank, mask=mask: e.matmul(
                    bank, c.identb_s, mask, start=True, stop=False), deps=dep0)
                lh = kbh[base:base + 64, c4h, 128 * tb:128 * tb + 128]
                llo = kbl[base:base + 64, c4h, 128 * tb:128 * tb + 128]
                rh = c.qfm_hi[base:base + 64, c4h, :]
                rl = c.qfm_lo[base:base + 64, c4h, :]
                P.emit("T", lambda e, bank=bank, lh=lh, rh=rh: e.matmul(
                    bank, lh, rh, start=False, stop=False))
                P.emit("T", lambda e, bank=bank, lh=lh, rl=rl: e.matmul(
                    bank, lh, rl, start=False, stop=False))
                P.emit("T", lambda e, bank=bank, llo=llo, rh=rh: e.matmul(
                    bank, llo, rh, start=False, stop=True))
            tqk = P.emit("T", lambda e: e.nop(), mark=True)
            te1 = P.emit("A", lambda e: e.activation(c.P_hi, c.PS_A[:, 0:2048],
                                                     AF.Exp), deps=[tqk, pv_prev],
                         mark=True)
            tvs = []
            for qtr in range(4):
                te2 = P.emit("A", lambda e, qtr=qtr: e.activation(
                    c.P_f, c.PS_A[:, 512 * qtr:512 * qtr + 512], AF.Exp),
                    deps=[tvs[-1]] if tvs else (), mark=True)
                tvs.append(P.emit("V", lambda e, qtr=qtr: e.tensor_tensor(
                    c.P_lo[:, 512 * qtr:512 * qtr + 512], c.P_f,
                    c.P_hi[:, 512 * qtr:512 * qtr + 512], op=OP.subtract),
                    deps=[te2], mark=True))
            exp_done = tvs[-1]
            for ci in range(4):
                cg = 4 * half + ci
                vbh = c.vprv_hi if cg < 4 else c.vcur_hi
                vbl = c.vprv_lo if cg < 4 else c.vcur_lo
                tb = cg % 4
                lh = vbh[:, tb, 65 * h:65 * h + 65]
                llo = vbl[:, tb, 65 * h:65 * h + 65]
                rh = c.P_hi[:, 512 * ci:512 * ci + 512]
                rl = c.P_lo[:, 512 * ci:512 * ci + 512]
                first = (half == 0 and ci == 0)
                last = (half == 1 and ci == 3)
                dep0 = [exp_done, ops_free[hb]] if first else [exp_done]
                mm3(P, opsum[0:65, :], lh, llo, rh, rl, first, last, deps=dep0)
            tpv = P.emit("T", lambda e: e.nop(), mark=True)
            pv_prev = tpv
        c.free["pv_prev"] = tpv
        ta_s = P.emit("A", lambda e, opsum=opsum: e.activation(
            c.sums2[64:65, 0:512], opsum[64:65, :], AF.Copy), deps=[tpv], mark=True)
        tv_r = P.emit("V", lambda e: e.reciprocal(
            c.sums2[64:65, 512:1024], c.sums2[64:65, 0:512]), deps=[ta_s], mark=True)
        ta_rh = P.emit("A", lambda e: e.activation(
            c.sums2b[64:65, 0:512], c.sums2[64:65, 512:1024], AF.Copy),
            deps=[tv_r], mark=True)
        tv_rl = P.emit("V", lambda e: e.tensor_tensor(
            c.sums2b[64:65, 512:1024], c.sums2[64:65, 512:1024],
            c.sums2b[64:65, 0:512], op=OP.subtract), deps=[ta_rh], mark=True)
        P.emit("T", lambda e: e.matmul(
            c.PS_B[:, 0:512], c.sel64_s, c.sums2b[:, 0:512], start=True, stop=False),
            deps=[tv_rl, psB_free])
        tb_ = P.emit("T", lambda e: e.matmul(
            c.PS_B[:, 0:512], c.sel64_s, c.sums2b[:, 512:1024], start=False, stop=True),
            mark=True)
        ta_r = P.emit("A", lambda e: e.activation(c.recipb, c.PS_B[:, 0:512], AF.Copy),
                      deps=[tb_], mark=True)
        tv_o1 = P.emit("V", lambda e, opsum=opsum: e.tensor_tensor(
            c.oscr, opsum[0:64, :], c.recipb[0:64, :], op=OP.mult),
            deps=[ta_r], mark=True)
        ta_o = P.emit("A", lambda e, hb=hb: e.activation(
            c.o_hi[:, hb, :], c.oscr, AF.Copy), deps=[tv_o1, o_free[hb]], mark=True)
        tv_o2 = P.emit("V", lambda e, hb=hb: e.tensor_tensor(
            c.o_lo[:, hb, :], c.oscr, c.o_hi[:, hb, :], op=OP.subtract),
            deps=[ta_o], mark=True)
        ops_free[hb] = tv_o1
        psB_free = ta_r
        tTw = None
        for e8 in range(8):
            wop = c.PS_B[:, 512:1024]
            lh = c.wow[:, hb, 0, 128 * e8:128 * e8 + 128]
            llo = c.wow[:, hb, 1, 128 * e8:128 * e8 + 128]
            dep0 = [twl, ta_o, tv_o2, wo_ps_free]
            mm3(P, wop, lh, llo, c.o_hi[:, hb, :], c.o_lo[:, hb, :],
                True, True, deps=dep0)
            tTw = P.emit("T", lambda e: e.nop(), mark=True)
            if h == 0:
                if l == 0:
                    basef = c.xn_f[:, e8, 0:512]
                else:
                    basef = c.o1_f[:, e8, 1:513]
                x1_last = P.emit("V", lambda e, e8=e8, basef=basef, wop=wop:
                                 e.tensor_tensor(c.x1_f[:, e8, 1:513], wop, basef,
                                                 op=OP.add), deps=[tTw], mark=True)
            else:
                x1_last = P.emit("V", lambda e, e8=e8, wop=wop: e.tensor_tensor(
                    c.x1_f[:, e8, 1:513], wop, c.x1_f[:, e8, 1:513],
                    op=OP.add), deps=[tTw], mark=True)
            wo_ps_free = x1_last
        wow_free[hb] = tTw
        o_free[hb] = tTw
    c.free["PS_A"] = exp_done
    c.free["PS_B"] = psB_free
    c.free["PS_C"] = ops_free[0]
    c.free["PS_D"] = ops_free[1]
    c.free["wow0"] = wow_free[0]
    c.free["wow1"] = wow_free[1]
    c.free[f"kspill_r{l}"] = c.free.get("pv_prev")
    c.free["mvar"] = exp_done
    tcol = P.emit("V", lambda e: e.tensor_copy(
        c.x1col[:, l, :], c.x1_f[:, :, 512:513].rearrange("p a o -> p (a o)")),
        deps=[x1_last], mark=True)
    if mode == "x1":
        return

    ta = P.emit("A", lambda e: e.activation(c.x1s_hi, c.x1_f, AF.Copy),
                deps=[x1_last, P.tok("T")], mark=True)
    tv = P.emit("V", lambda e: e.tensor_tensor(c.x1s_lo, c.x1_f, c.x1s_hi,
                                               op=OP.subtract), deps=[ta], mark=True)
    nmm = 0
    sq_done = []
    tmm = None
    for d in range(8):
        off = 0 if d >= 4 else 1
        for dt in range(2):
            buf = c.x1s_hi if dt == 0 else c.x1s_lo
            P.emit("T", lambda e, buf=buf, d=d, off=off, nmm=nmm: e.matmul(
                c.PS_A[0:1, 0:512], c.ones_s[:, 0:1], buf[:, d, off:off + 512],
                start=(nmm == 0), stop=(nmm == 15)),
                deps=[tv, c.free.get("PS_A")] if nmm == 0 else ())
            nmm += 1
        xsl = c.x1_f[:, d, off:off + 512]
        tsq = P.emit("V", lambda e, xsl=xsl: e.tensor_tensor(
            c.gscr, xsl, xsl, op=OP.mult), deps=[x1_last, tmm, P.tok("A")],
            mark=True)
        tsh = P.emit("A", lambda e: e.activation(c.sqh, c.gscr, AF.Copy),
                     deps=[tsq], mark=True)
        tsl = P.emit("V", lambda e: e.tensor_tensor(c.sql, c.gscr, c.sqh,
                                                    op=OP.subtract), deps=[tsh],
                     mark=True)
        for dt in range(2):
            buf = c.sqh if dt == 0 else c.sql
            tmm = P.emit("T", lambda e, buf=buf, d=d, dt=dt: e.matmul(
                c.PS_A[0:1, 512:1024], c.ones_s[:, 0:1], buf,
                start=(d == 0 and dt == 0), stop=(d == 7 and dt == 1)),
                deps=[tsl], mark=True)
        sq_done.append(tsl)
    tstat = P.tok("T")
    inv = 1.0 / 1024.0
    P.emit("V", lambda e: e.tensor_scalar_mul(c.statr[0:1, 0, :], c.PS_A[0:1, 0:512],
                                              inv), deps=[tstat, P.tok("A")])
    P.emit("V", lambda e: e.tensor_scalar_mul(c.gscr[0:1, :],
                                              c.PS_A[0:1, 512:1024], inv))
    P.emit("V", lambda e: e.tensor_tensor(c.gscr2[0:1, :], c.statr[0:1, 0, :],
                                          c.statr[0:1, 0, :], op=OP.mult))
    P.vdrain()
    tvv = P.emit("V", lambda e: e.tensor_tensor(c.gscr[0:1, :], c.gscr[0:1, :],
                                                c.gscr2[0:1, :], op=OP.subtract),
                 mark=True)
    tas = P.emit("A", lambda e: e.activation(c.gscr2[0:1, :], c.gscr[0:1, :],
                                             AF.Sqrt, bias=c.eps_s[0:1, :]),
                 deps=[tvv], mark=True)
    P.emit("V", lambda e: e.reciprocal(c.statr[0:1, 1, :], c.gscr2[0:1, :]),
           deps=[tas])
    P.emit("V", lambda e: e.tensor_scalar(c.gscr[0:1, :], c.gscr[0:1, :],
                                          c.eps_s[0:1, :], None, op0=OP.add))
    P.emit("V", lambda e: e.tensor_tensor(c.gscr2[0:1, :], c.statr[0:1, 1, :],
                                          c.statr[0:1, 1, :], op=OP.mult))
    P.vdrain()
    P.emit("V", lambda e: e.tensor_tensor(c.gscr2[0:1, :], c.gscr[0:1, :],
                                          c.gscr2[0:1, :], op=OP.mult))
    P.vdrain()
    P.emit("V", lambda e: e.tensor_scalar(c.gscr2[0:1, :], c.gscr2[0:1, :],
                                          c.barsc[0:1, 3:4], c.barsc[0:1, 4:5],
                                          op0=OP.mult, op1=OP.add))
    P.vdrain()
    tvr = P.emit("V", lambda e: e.tensor_tensor(c.statr[0:1, 1, :],
                                                c.statr[0:1, 1, :],
                                                c.gscr2[0:1, :], op=OP.mult),
                 mark=True)
    tah = P.emit("A", lambda e: e.activation(c.statb[0:1, 0, 0:512],
                                             c.statr[0:1, 0, :], AF.Copy),
                 deps=[tvr], mark=True)
    tah2 = P.emit("A", lambda e: e.activation(c.statb[0:1, 0, 512:1024],
                                              c.statr[0:1, 1, :], AF.Copy), mark=True)
    tvl = P.emit("V", lambda e: e.tensor_tensor(c.statb[0:1, 1, 0:512],
                                                c.statr[0:1, 0, :],
                                                c.statb[0:1, 0, 0:512],
                                                op=OP.subtract), deps=[tah], mark=True)
    tvl2 = P.emit("V", lambda e: e.tensor_tensor(c.statb[0:1, 1, 512:1024],
                                                 c.statr[0:1, 1, :],
                                                 c.statb[0:1, 0, 512:1024],
                                                 op=OP.subtract), deps=[tah2],
                  mark=True)
    for hblk in range(2):
        P.emit("T", lambda e, hblk=hblk: e.matmul(
            c.PS_B[:, 512 * hblk:512 * hblk + 512], c.ones_s[0:1, :],
            c.statb[0:1, 0, 512 * hblk:512 * hblk + 512], start=True, stop=False),
            deps=[tvl2, tvl, c.free.get("PS_B")])
        P.emit("T", lambda e, hblk=hblk: e.matmul(
            c.PS_B[:, 512 * hblk:512 * hblk + 512], c.ones_s[0:1, :],
            c.statb[0:1, 1, 512 * hblk:512 * hblk + 512], start=False, stop=True))
    tbb = P.emit("T", lambda e: e.nop(), mark=True)
    h2_done = []
    tah = None
    for d in range(8):
        off = 0 if d >= 4 else 1
        xsl = c.x1_f[:, d, off:off + 512]
        P.emit("V", lambda e, xsl=xsl: e.tensor_tensor(
            c.gscr, xsl, c.PS_B[:, 0:512], op=OP.subtract),
            deps=[tbb, sq_done[-1], tah, P.tok("A")])
        P.emit("V", lambda e: e.tensor_tensor(c.gscr2, c.gscr, c.PS_B[:, 512:1024],
                                              op=OP.mult))
        tv3 = P.emit("V", lambda e, d=d: e.tensor_scalar(
            c.gscr, c.gscr2, c.lng_s[:, l, d:d + 1], c.lnb_s[:, l, d:d + 1],
            op0=OP.mult, op1=OP.add), mark=True)
        tah = P.emit("A", lambda e, d=d: e.activation(c.h2_hi[:, d, :], c.gscr,
                                                      AF.Copy), deps=[tv3], mark=True)
        h2_done.append(P.emit("V", lambda e, d=d: e.tensor_tensor(
            c.h2_lo[:, d, :], c.gscr, c.h2_hi[:, d, :], op=OP.subtract),
            deps=[tah], mark=True))
    h2tok = h2_done[-1]

    grps = [(0, 6), (6, 6), (12, 6), (18, 4)]
    w1b = 0
    w1_free = [c.free.get("w1w0"), c.free.get("w1w1")]
    ps_free = [c.free.get("PS_C"), c.free.get("PS_D")]
    w2_read = None
    h4_free = [None] * 6
    out_ev = [None, None]
    for (k0, kn) in grps:
        tw2h = P.dma("S", c.w2b_hi[:, 0:kn, :], c.hw2p[l, 128 * k0:128 * (k0 + kn), 0, :]
                     .rearrange("(cc p) e -> p cc e", p=128),
                     deps=[w2_read, tstat, P.tok("A"), P.tok("V")], stream="w2")
        tw2l = P.dma("S", c.w2b_lo[:, 0:kn, :], c.hw2p[l, 128 * k0:128 * (k0 + kn), 1, :]
                     .rearrange("(cc p) e -> p cc e", p=128), stream="w2")
        h4_done = []
        tvm_prev = None
        for mi in range(kn):
            m = k0 + mi
            for part, ps in ((0, c.PS_C), (1, c.PS_D)):
                tw1 = P.dma("S",
                            c.w1w[:, w1b].rearrange("p dc dt col -> p dc (dt col)"),
                            c.hw1p[l, 22 * part + m]
                            .rearrange("(dc p) dt col -> p dc (dt col)", p=128),
                            deps=[w1_free[w1b]], stream=f"w1{w1b}")
                for d in range(8):
                    first = (d == 0)
                    last = (d == 7)
                    dep0 = [tw1, h2tok, ps_free[part]] if first else ()
                    mm3(P, ps, c.w1w[:, w1b, d, 0, :], c.w1w[:, w1b, d, 1, :],
                        c.h2_hi[:, d, :], c.h2_lo[:, d, :], first, last, deps=dep0)
                tT = P.emit("T", lambda e: e.nop(), mark=True)
                w1_free[w1b] = tT
                w1b ^= 1
                if part == 0:
                    tta = tT
                else:
                    ttg = tT
            ta_a = P.emit("A", lambda e, m=m: e.activation(
                c.gscr2, c.PS_C, AF.Identity, bias=c.b1_s[:, l, 0, m:m + 1]),
                deps=[tta, tvm_prev], mark=True)
            ta_g = P.emit("A", lambda e, m=m: e.activation(
                c.gscr, c.PS_D, AF.Gelu, bias=c.b1_s[:, l, 1, m:m + 1]),
                deps=[ttg], mark=True)
            ps_free[0] = ta_a
            ps_free[1] = ta_g
            tv_m = P.emit("V", lambda e: e.tensor_tensor(c.gscr2, c.gscr2, c.gscr,
                                                         op=OP.mult),
                          deps=[ta_a, ta_g], mark=True)
            tvm_prev = tv_m
            ta_h = P.emit("A", lambda e, mi=mi: e.activation(
                c.h4_hi[:, mi, :], c.gscr2, AF.Copy),
                deps=[tv_m, h4_free[mi]], mark=True)
            tv_h = P.emit("V", lambda e, mi=mi: e.tensor_tensor(
                c.h4_lo[:, mi, :], c.gscr2, c.h4_hi[:, mi, :], op=OP.subtract),
                deps=[ta_h], mark=True)
            h4_done.append(tv_h)
        for e8 in range(8):
            pi = e8 % 2
            ps = c.PS_C if pi == 0 else c.PS_D
            for mi in range(kn):
                first = (mi == 0)
                last = (mi == kn - 1)
                dep0 = [tw2h, tw2l, h4_done[-1], ps_free[pi]] if first else ()
                mm3(P, ps, c.w2b_hi[:, mi, 128 * e8:128 * e8 + 128],
                    c.w2b_lo[:, mi, 128 * e8:128 * e8 + 128],
                    c.h4_hi[:, mi, :], c.h4_lo[:, mi, :], first, last, deps=dep0)
            tT = P.emit("T", lambda e: e.nop(), mark=True)
            w2_read = tT
            if l == 0:
                outv = c.o1_f[:, e8, 1:513]
            else:
                outv = c.xn_f[:, e8, 0:512]
            gidx = grps.index((k0, kn))
            if gidx == 0:
                tv = P.emit("V", lambda e, outv=outv, ps=ps, e8=e8: e.tensor_tensor(
                    outv, ps, c.x1_f[:, e8, 1:513], op=OP.add), deps=[tT], mark=True)
            elif gidx == len(grps) - 1:
                tv = P.emit("V", lambda e, outv=outv, ps=ps, e8=e8: e.scalar_tensor_tensor(
                    outv, ps, c.b2_s[:, l, e8:e8 + 1], outv, op0=OP.add, op1=OP.add),
                    deps=[tT], mark=True)
            else:
                tv = P.emit("V", lambda e, outv=outv, ps=ps: e.tensor_tensor(
                    outv, ps, outv, op=OP.add), deps=[tT], mark=True)
            ps_free[pi] = tv
            out_ev[pi] = tv
        for i in range(6):
            h4_free[i] = w2_read
    c.free["PS_C"] = ps_free[0]
    c.free["PS_D"] = ps_free[1]
    c.free["w1w0"] = w1_free[0]
    c.free["w1w1"] = w1_free[1]
    c.free["w2b"] = w2_read
    c.free["PS_A"] = tstat
    c.free["PS_B"] = h2_done[-1]
    if l == 0:
        P.emit("V", lambda e: e.tensor_copy(
            c.o1col, c.o1_f[:, :, 512:513].rearrange("p a o -> p (a o)")),
            deps=[out_ev[0], out_ev[1]], mark=True)
        c.free["o1_done"] = P.tok("V")
    else:
        c.free["out2_done"] = P.tok("V")


def emit_final(P, c, k):
    g = c.geo
    psA_free = c.free.get("PS_A")
    tv = None
    for t in range(4):
        for d in range(8):
            P.emit("T", lambda e, t=t, d=d: e.transpose(
                c.PS_A[:, 128 * d:128 * d + 128], c.xn_f[:, d, 128 * t:128 * t + 128],
                c.identf_s),
                deps=[c.free.get("out2_done"), psA_free] if d == 0 else ())
        tT = P.emit("T", lambda e: e.nop(), mark=True)
        psv = c.PS_A[:, 0:1024]
        tv = P.emit("V", lambda e, t=t, psv=psv: e.tensor_tensor(
            c.acc[:, t, :], c.acc[:, t, :], psv, op=OP.add),
            deps=[tT, c.free.get("acc_ev")], mark=True)
        psA_free = tv
    c.free["PS_A"] = psA_free
    tst = P.dma("G", c.out[512 * k:512 * k + 512, :].rearrange(
        "(a p) e -> p a e", p=128), c.acc, deps=[tv], stream="out")
    c.free["acc"] = tst


def emit_all(P, c):
    g = c.geo
    emit_startup(P, c)
    limit = globals().get("STRIPE_LIMIT")
    stripes = g.stripes if limit is None else g.stripes[:limit]
    for k in stripes:
        P.barrier()
        toff = g.t0_off(k)
        tabs_tok = None
        if not globals().get("NO_TABS"):
            for i, tbl in enumerate((c.Aq, c.Bq, c.Ak, c.Bk)):
                tabs_tok = P.dma("S", c.tabs[:, :, i, :],
                                 tbl[toff:toff + 512, :].rearrange("(a p) s -> p a s", p=128),
                                 stream="tabs")
        c.free["tabs_tok"] = tabs_tok
        emit_ph0(P, c, k)
        if globals().get("PH0_ONLY"):
            continue
        if k >= 0:
            emit_ph1(P, c, k)
        if not globals().get("NO_FFN"):
            emit_ffn(P, c, k, 0, "kv" if k == -g.warm else "full")
            if k >= -2:
                emit_ffn(P, c, k, 1,
                         {-2: "kv", -1: "x1"}.get(k, "full"))
        if k >= 0:
            emit_final(P, c, k)
    P.barrier(with_dma=True)



_CACHE = {}


def _get_program(geo_key):
    if geo_key not in _CACHE:
        geo = Geo(*geo_key)
        nc, P, c = build_program(geo)
        emit_all(P, c)
        P.finalize_block()
        _CACHE[geo_key] = (geo, nc)
    return _CACHE[geo_key]


def kernel(**inputs):
    x = np.asarray(inputs["L0_attn_query"])
    B, T, D = x.shape
    geo, nc = _get_program((T, B, 8))
    in_maps = stage_inputs(geo, inputs)
    res = run_bass_kernel_spmd(nc, in_maps, core_ids=list(range(geo.n_cores)))
    out = np.empty((B, T, D), dtype=np.float32)
    for ci in range(geo.n_cores):
        b, s = geo.core_pos(ci)
        out[b, s:s + geo.own] = res.results[ci]["out"]
    return out


# version 03 (final)
# speedup vs baseline: 1.0169x; 1.0169x over previous
import math
import numpy as np
import ml_dtypes
import concourse.bass as bass
import concourse.mybir as mybir
from concourse.bass_utils import run_bass_kernel_spmd

F32 = mybir.dt.float32
BF16 = mybir.dt.bfloat16
AF = mybir.ActivationFunctionType
OP = mybir.AluOpType
AX = mybir.AxisListType
BF16NP = ml_dtypes.bfloat16

NEG = -3.0e38


class Geo:
    def __init__(self, T_seq=16384, B=2, n_cores=8):
        self.T_seq = T_seq
        self.B = B
        self.n_cores = n_cores
        assert (T_seq * B) % n_cores == 0
        self.cores_per_b = n_cores // B
        self.own = T_seq // self.cores_per_b
        self.W = 512
        assert self.own % self.W == 0
        self.n_own = self.own // self.W
        self.warm = 3
        self.LH = self.warm * self.W + 128
        self.RH = 128
        self.Tc = self.LH + self.own + self.RH
        self.stripes = list(range(-self.warm, self.n_own))

    def core_pos(self, c):
        b = c // self.cores_per_b
        s = (c % self.cores_per_b) * self.own
        return b, s

    def t0_off(self, k):
        return self.LH + k * self.W



def _rotary_tables(n, t_lo, t_hi):
    DIM_HEAD, WINDOW, QK_SCALE = 64, 512, 8.0
    half = np.arange(0, DIM_HEAD, 2, dtype=np.float64)
    inv_freq = 1.0 / 10000.0 ** (half / DIM_HEAD)
    t = np.arange(t_lo, t_hi, dtype=np.float64)
    freqs = t[:, None] * inv_freq[None, :]
    freqs = np.concatenate([freqs, freqs], axis=-1)
    base = (half + 0.4 * DIM_HEAD) / (1.4 * DIM_HEAD)
    power = (t - n // 2) / float(WINDOW // 2)
    scale = base[None, :] ** power[:, None]
    scale = np.concatenate([scale, scale], axis=-1)
    cos, sin = np.cos(freqs), np.sin(freqs)
    A_q = cos * scale
    B_q = sin * scale
    A_k = cos / scale
    B_k = sin / scale
    def sgn(Bt):
        Bt = Bt.copy()
        Bt[:, 0:32] *= -1.0
        return Bt
    f32 = np.float32
    return (A_q.astype(f32) * np.float32(QK_SCALE),
            sgn(B_q).astype(np.float64).astype(f32) * np.float32(QK_SCALE),
            A_k.astype(f32), sgn(B_k).astype(f32))


def _band_masks():
    kp = np.arange(128)[:, None]
    qp = np.arange(512)[None, :]
    out = np.zeros((8, 128, 512), dtype=np.float32)
    for c in range(8):
        allowed = (qp >= 128 * c + kp - 512) & (qp <= 128 * c + kp)
        out[c] = np.where(allowed, 0.0, NEG)
    return out.astype(BF16NP)


def stage_inputs(geo, inputs):
    g = geo
    x_full = np.asarray(inputs["L0_attn_query"], dtype=np.float32)
    masks8 = _band_masks()

    W_inT = np.ascontiguousarray(np.asarray(inputs["L0_attn_W_in"], np.float32).T)
    W_attnT = np.ascontiguousarray(np.asarray(inputs["L0_attn_W_attn"], np.float32).T)
    W_outT = np.ascontiguousarray(np.asarray(inputs["L0_attn_W_out"], np.float32).T)
    wqkv = np.ascontiguousarray(np.asarray(inputs["L0_ffn_wqkv"], np.float32))
    wo_h = np.ascontiguousarray(
        np.asarray(inputs["L0_ffn_wo"], np.float32).reshape(2, 8, 64, 1024))
    w1 = np.asarray(inputs["L0_ffn_w1"], np.float32)
    HID = 2730
    w1a = w1[:, :, :HID]
    w1g = w1[:, :, HID:]
    pad = 22 * 128 - HID
    w1p = np.concatenate([
        np.pad(w1a, ((0, 0), (0, 0), (0, pad))),
        np.pad(w1g, ((0, 0), (0, 0), (0, pad)))], axis=2)
    w1p = np.ascontiguousarray(w1p)
    w2 = np.asarray(inputs["L0_ffn_w2"], np.float32)
    w2p = np.ascontiguousarray(np.pad(w2, ((0, 0), (0, pad), (0, 0))))
    b1 = np.asarray(inputs["L0_ffn_b1"], np.float32)
    b1p = np.stack([np.pad(b1[:, :HID], ((0, 0), (0, pad))),
                    np.pad(b1[:, HID:], ((0, 0), (0, pad)))], axis=1)
    b2 = np.asarray(inputs["L0_ffn_b2"], np.float32)
    ln_g = np.asarray(inputs["L0_ffn_ln_g"], np.float32)
    ln_b = np.asarray(inputs["L0_ffn_ln_b"], np.float32)
    b1p_t = np.ascontiguousarray(b1p.reshape(2, 2, 22, 128).transpose(3, 0, 1, 2))
    b2_t = np.ascontiguousarray(b2.reshape(2, 8, 128).transpose(2, 0, 1))
    lng_t = np.ascontiguousarray(ln_g.reshape(2, 8, 128).transpose(2, 0, 1))
    lnb_t = np.ascontiguousarray(ln_b.reshape(2, 8, 128).transpose(2, 0, 1))
    qsc = np.asarray(inputs["L0_ffn_q_scale"], np.float32)
    ksc = np.asarray(inputs["L0_ffn_k_scale"], np.float32)
    sw = np.asarray(inputs["L0_attn_scale_weights"], np.float32)

    ident_f = np.eye(128, dtype=np.float32)
    ident_b = np.eye(128, dtype=BF16NP)
    seld = np.zeros((128, 41, 128), dtype=BF16NP)
    for j in range(41):
        seld[j, j, :] = 1
        seld[64 + j, j, :] = 1
    sel64 = np.zeros((65, 128), dtype=BF16NP)
    sel64[64, :] = 1
    onesc = np.ones((128, 128), dtype=BF16NP)

    in_maps = []
    for c in range(g.n_cores):
        b, s = g.core_pos(c)
        xc = np.zeros((g.Tc, 1024), dtype=np.float32)
        lo = s - g.LH
        hi = s + g.own + g.RH
        src_lo, src_hi = max(lo, 0), min(hi, g.T_seq)
        xc[src_lo - lo: src_hi - lo] = x_full[b, src_lo:src_hi]
        tpos_lo, tpos_hi = lo, hi
        Aq, Bq, Ak, Bk = _rotary_tables(g.T_seq, tpos_lo, tpos_hi)
        for Aarr in (Aq, Bq, Ak, Bk):
            assert np.all(np.isfinite(Aarr))
        att_stripes = list(range(-2, g.n_own))
        mvar = np.zeros((len(att_stripes), 4, 128, 512), dtype=BF16NP)
        for i, k in enumerate(att_stripes):
            if s + k * g.W == 0:
                mvar[i] = np.full((4, 128, 512), NEG, dtype=BF16NP)
            else:
                mvar[i] = masks8[:4]
        in_maps.append({
            "x": xc,
            "Aq": Aq, "Bq": Bq, "Ak": Ak, "Bk": Bk,
            "mstat": np.ascontiguousarray(masks8[4:]),
            "mvar": mvar,
            "identf": ident_f, "identb": ident_b,
            "seld": seld, "sel64": sel64, "onesc": onesc,
            "WinT": W_inT, "WattnT": W_attnT, "WoutT": W_outT,
            "wqkv": wqkv, "woh": wo_h, "w1p": w1p, "w2p": w2p,
            "b1p": b1p_t, "b2": b2_t, "ln_g": lng_t, "ln_b": lnb_t,
            "qsc": qsc, "ksc": ksc, "sw": sw,
        })
    return in_maps



class Prog:
    EK = ("T", "V", "A", "G", "S")
    REAL_BARRIER = True
    MARK_DRAIN = True
    MARK_DRAIN2 = True
    MARK_DRAIN_EK = ("V", "A", "G", "S")

    def __init__(self, nc):
        self.nc = nc
        self.sems = {k: nc.alloc_semaphore(f"sem{k}") for k in self.EK}
        self.dsems = {"S": nc.alloc_semaphore("dmaS"), "G": nc.alloc_semaphore("dmaG")}
        self.q = {k: [] for k in self.EK}
        self.cnt = {k: 0 for k in self.EK}
        self.dcnt = {"S": 0, "G": 0}
        self.hw = {}

    def _dstream(self, key):
        if key not in self.dsems:
            self.dsems[key] = self.nc.alloc_semaphore(f"dma{key}")
            self.dcnt[key] = 0
        return self.dsems[key]

    def _wait(self, ek, dep):
        pk, val = dep
        key = (ek, pk)
        if self.hw.get(key, 0) >= val:
            return
        self.hw[key] = val
        sem = self.dsems[pk[1:]] if pk[0] == "d" else self.sems[pk]
        self.q[ek].append(lambda e, sem=sem, val=val: e.wait_ge(sem, val))

    def emit(self, ek, fn, deps=(), mark=False):
        for d in deps:
            if d is not None:
                self._wait(ek, d)
        if mark:
            self.cnt[ek] += 1
            sem = self.sems[ek]

            drain = self.MARK_DRAIN if ek in self.MARK_DRAIN_EK else self.MARK_DRAIN2
            if mark == "free":
                drain = False
            if drain:
                def thunk(e, fn=fn, sem=sem):
                    fn(e)
                    e.drain(fusable=True).then_inc(sem, 1)
            else:
                def thunk(e, fn=fn, sem=sem):
                    fn(e).then_inc(sem, 1)
            self.q[ek].append(thunk)
            return (ek, self.cnt[ek])
        self.q[ek].append(fn)
        return None

    def tok(self, ek):
        return (ek, self.cnt[ek])

    def vdrain(self):
        self.q["V"].append(lambda e: e.drain())

    def dma(self, qk, out_ap, in_ap, deps=(), stream=None):
        for d in deps:
            if d is not None:
                self._wait(qk, d)
        key = stream or qk
        dsem = self._dstream(key)
        self.dcnt[key] += 1
        self.q[qk].append(
            lambda e, o=out_ap, i=in_ap, dsem=dsem:
                e.dma_start(out=o, in_=i).then_inc(dsem, 16))
        return ("d" + key, 16 * self.dcnt[key])

    def barrier(self, with_dma=False):
        if with_dma:
            for k in self.EK:
                for key, cntv in self.dcnt.items():
                    self._wait(k, ("d" + key, 16 * cntv))
        self.finalize_block()

    def finalize_block(self):
        nc = self.nc
        if not any(self.q[k] for k in self.EK):
            return
        with nc.Block() as block:
            qs = {k: list(self.q[k]) for k in self.EK}

            def play(key):
                def f(e):
                    for t in qs[key]:
                        t(e)
                return f
            block.sync(play("S"))
            block.tensor(play("T"))
            block.vector(play("V"))
            block.scalar(play("A"))
            block.gpsimd(play("G"))
        for k in self.EK:
            self.q[k] = []



class Ctx:
    pass


def _carve(nc, zb, base, name, shape, dtype, cur):
    h = nc.alloc_sbuf_tensor_at(name, shape, dtype, offset=base + cur)
    per = 1
    for s in shape[1:]:
        per *= s
    per *= mybir.dt.size(dtype)
    per = (per + 63) // 64 * 64
    return h.ap(), cur + per


def build_program(geo):
    g = geo
    nc = bass.Bass("TRN2", target_bir_lowering=False,
                   detect_race_conditions=False)
    P = Prog(nc)
    c = Ctx()
    c.geo = g
    c.nc = nc

    def din(name, shape, dtype=F32):
        return nc.dram_tensor(name, list(shape), dtype, kind="ExternalInput").ap()

    c.x = din("x", (g.Tc, 1024))
    c.Aq = din("Aq", (g.Tc, 64)); c.Bq = din("Bq", (g.Tc, 64))
    c.Ak = din("Ak", (g.Tc, 64)); c.Bk = din("Bk", (g.Tc, 64))
    c.mstat = din("mstat", (4, 128, 512), BF16)
    n_att = g.n_own + 2
    c.mvar = din("mvar", (n_att, 4, 128, 512), BF16)
    c.identf = din("identf", (128, 128))
    c.identb = din("identb", (128, 128), BF16)
    c.seld = din("seld", (128, 41, 128), BF16)
    c.sel64 = din("sel64", (65, 128), BF16)
    c.onesc = din("onesc", (128, 128), BF16)
    c.WinT = din("WinT", (1024, 1024))
    c.WattnT = din("WattnT", (1024, 656))
    c.WoutT = din("WoutT", (1024, 1024))
    c.wqkv = din("wqkv", (2, 1024, 1536))
    c.woh = din("woh", (2, 8, 64, 1024))
    c.w1p = din("w1p", (2, 1024, 5632))
    c.w2p = din("w2p", (2, 2816, 1024))
    c.b1p = din("b1p", (128, 2, 2, 22))
    c.b2 = din("b2", (128, 2, 8))
    c.ln_g = din("ln_g", (128, 2, 8)); c.ln_b = din("ln_b", (128, 2, 8))
    c.qsc = din("qsc", (2, 64)); c.ksc = din("ksc", (2, 64))
    c.sw = din("sw", (2,))
    c.out = nc.dram_tensor("out", [g.own, 1024], F32, kind="ExternalOutput").ap()

    def dint(name, shape, dtype=BF16):
        return nc.dram_tensor(name, list(shape), dtype, kind="Internal").ap()

    c.hWinT = dint("hWinT", (8, 1024, 2, 128))
    c.hWattnT = dint("hWattnT", (1024, 2, 656))
    c.hWoutT = dint("hWoutT", (1024, 2, 1024))
    c.hwqkv = dint("hwqkv", (2, 1024, 2, 1536))
    c.hwoh = dint("hwoh", (2, 8, 64, 2, 1024))
    c.hw1p = dint("hw1p", (2, 44, 1024, 2, 128))
    c.hw2p = dint("hw2p", (2, 2816, 2, 1024))
    nslot = g.n_own + g.warm + 1
    c.kspill = dint("kspill", (2, nslot, 2, 512, 512))
    c.vspill = dint("vspill", (2, nslot, 2, 512, 520))

    c.PS_A = nc.alloc_psum_tensor("PS_A", [128, 2048], F32).ap()
    c.PS_B = nc.alloc_psum_tensor("PS_B", [128, 1024], F32).ap()
    c.PS_C = nc.alloc_psum_tensor("PS_C", [128, 512], F32).ap()
    c.PS_D = nc.alloc_psum_tensor("PS_D", [128, 512], F32).ap()

    A = nc.alloc_sbuf_tensor
    c.identf_s = A("identf_s", [128, 128], F32).ap()
    c.identb_s = A("identb_s", [128, 128], BF16).ap()
    c.seld_s = A("seld_s", [128, 41, 128], BF16).ap()
    c.sel64_s = A("sel64_s", [65, 128], BF16).ap()
    c.ones_s = A("ones_s", [128, 128], BF16).ap()
    c.mstat_s = A("mstat_s", [128, 4, 512], BF16).ap()
    c.mvar_s = A("mvar_s", [128, 4, 512], BF16).ap()
    c.tabs = A("tabs", [128, 4, 4, 64], F32).ap()
    c.tabe = A("tabe", [128, 2, 64], F32).ap()
    c.qs_s = A("qs_s", [128, 2, 2, 64], F32).ap()
    c.ks_s = A("ks_s", [128, 2, 2, 64], F32).ap()
    c.lng_s = A("lng_s", [128, 2, 8], F32).ap()
    c.lnb_s = A("lnb_s", [128, 2, 8], F32).ap()
    c.b2_s = A("b2_s", [128, 2, 8], F32).ap()
    c.b1_s = A("b1_s", [128, 2, 2, 22], F32).ap()
    c.sw_s = A("sw_s", [128, 2], F32).ap()
    c.eps_s = A("eps_s", [128, 1], F32).ap()
    c.x1col = A("x1col", [128, 2, 8], F32).ap()
    c.o1col = A("o1col", [128, 8], F32).ap()
    c.sums2 = A("sums2", [65, 1024], F32).ap()
    c.sums2b = A("sums2b", [65, 1024], BF16).ap()
    c.cols = A("cols", [128, 16], F32).ap()
    c.barsc = A("barsc", [128, 5], F32).ap()
    c.barscb = A("barscb", [128, 1], BF16).ap()
    P.barscb = c.barscb
    P.barsc = c.barsc

    c.xn_f = A("xn_f", [128, 8, 512], F32).ap()
    c.xn_hi = A("xn_hi", [128, 8, 768], BF16).ap()
    c.xn_lo = A("xn_lo", [128, 8, 768], BF16).ap()
    c.x1_f = A("x1_f", [128, 8, 513], F32).ap()
    c.o1_f = A("o1_f", [128, 8, 513], F32).ap()
    c.acc = A("acc", [128, 4, 1024], F32).ap()
    c.brow = A("brow", [1, 256], BF16).ap()
    c.qkvw = A("qkvw", [128, 2, 2, 512], BF16).ap()
    c.w1w = A("w1w", [128, 2, 8, 2, 128], BF16).ap()
    mwh = A("mw", [128, 2, 8, 2, 128], BF16)
    c.mw = mwh.ap()
    mwbase = nc.lookup_mloc(mwh).addr
    c.wow = nc.alloc_sbuf_tensor_at("wow", [64, 2, 2, 1024], BF16,
                                    offset=mwbase).ap()

    ZBYTES = 68064
    zb_h = A("ZB", [128, ZBYTES], mybir.dt.uint8)
    zbase = nc.lookup_mloc(zb_h).addr
    cur = 0
    def CV(name, shape, dtype):
        nonlocal cur
        ap, cur2 = _carve(nc, zb_h, zbase, name, shape, dtype, cur)
        cur = cur2
        return ap
    c.qscr = CV("qscr", [128, 512], F32)
    c.qscr2 = CV("qscr2", [128, 512], F32)
    c.qfm_hi = CV("qfm_hi", [128, 4, 512], BF16)
    c.qfm_lo = CV("qfm_lo", [128, 4, 512], BF16)
    c.kcur_hi = CV("kcur_hi", [128, 4, 512], BF16)
    c.kcur_lo = CV("kcur_lo", [128, 4, 512], BF16)
    c.vcur_hi = CV("vcur_hi", [128, 4, 520], BF16)
    c.vcur_lo = CV("vcur_lo", [128, 4, 520], BF16)
    c.kprv_hi = CV("kprv_hi", [128, 4, 512], BF16)
    c.kprv_lo = CV("kprv_lo", [128, 4, 512], BF16)
    c.vprv_hi = CV("vprv_hi", [128, 4, 520], BF16)
    c.vprv_lo = CV("vprv_lo", [128, 4, 520], BF16)
    c.P_hi = CV("P_hi", [128, 2048], BF16)
    c.P_lo = CV("P_lo", [128, 2048], BF16)
    c.P_f = CV("P_f", [128, 512], F32)
    c.o_hi = CV("o_hi", [64, 2, 512], BF16)
    c.o_lo = CV("o_lo", [64, 2, 512], BF16)
    c.oscr = CV("oscr", [64, 512], F32)
    c.recipb = CV("recipb", [128, 512], F32)
    attn_end = cur
    cur = 0
    c.zx = CV("zx", [128, 1024], F32)
    c.zxn = CV("zxn", [128, 1024], F32)
    c.a_f = CV("a_f", [128, 8, 768], F32)
    c.rj_hi = CV("rj_hi", [128, 2, 128], BF16)
    c.rj_lo = CV("rj_lo", [128, 2, 128], BF16)
    c.w_tm = CV("w_tm", [128, 4, 520], F32)
    c.wfm2 = CV("wfm2", [128, 4, 512], BF16)
    c.dtmp = CV("dtmp", [128, 1024], F32)
    c.dyn_f = CV("dyn_f", [128, 8, 512], F32)
    c.dj_hi = CV("dj_hi", [128, 2, 128], BF16)
    c.dj_lo = CV("dj_lo", [128, 2, 128], BF16)
    ph1_end = cur
    cur = 0
    c.h2_hi = CV("h2_hi", [128, 8, 512], BF16)
    c.h2_lo = CV("h2_lo", [128, 8, 512], BF16)
    c.h4_hi = CV("h4_hi", [128, 6, 512], BF16)
    c.h4_lo = CV("h4_lo", [128, 6, 512], BF16)
    W2B_OFF = cur
    c.w2b_hi = CV("w2b_hi", [128, 6, 1024], BF16)
    c.w2b_lo = CV("w2b_lo", [128, 6, 1024], BF16)
    c.gscr = CV("gscr", [128, 512], F32)
    c.gscr2 = CV("gscr2", [128, 512], F32)
    c.sqh = CV("sqh", [128, 512], BF16)
    c.sql = CV("sql", [128, 512], BF16)
    c.statr = CV("statr", [1, 2, 512], F32)
    c.statb = CV("statb", [1, 2, 1024], BF16)
    mlp_end = cur
    c.x1s_hi, _ = _carve(nc, zb_h, zbase, "x1s_hi", [128, 8, 513], BF16, W2B_OFF)
    c.x1s_lo, _ = _carve(nc, zb_h, zbase, "x1s_lo", [128, 8, 513], BF16, W2B_OFF + 8256)
    c.o1s_hi, _ = _carve(nc, zb_h, zbase, "o1s_hi", [128, 8, 513], BF16, W2B_OFF)
    c.o1s_lo, _ = _carve(nc, zb_h, zbase, "o1s_lo", [128, 8, 513], BF16, W2B_OFF + 8256)
    assert attn_end <= ZBYTES and ph1_end <= ZBYTES and mlp_end <= ZBYTES, (
        attn_end, ph1_end, mlp_end)
    c._zb = (zb_h, zbase)

    c.free = {}
    c.P = P
    return nc, P, c



def mm3(P, ps, lh, ll, rh, rl, start, stop, deps=(), mark=False):
    if globals().get("MM_HIONLY"):
        return P.emit("T", lambda e: e.matmul(ps, lh, rh, start=start, stop=stop),
                      deps=deps, mark=mark)
    P.emit("T", lambda e: e.matmul(ps, lh, rh, start=start, stop=False), deps=deps)
    P.emit("T", lambda e: e.matmul(ps, lh, rl, start=False, stop=False))
    return P.emit("T", lambda e: e.matmul(ps, ll, rh, start=False, stop=stop),
                  mark=mark)


def emit_startup(P, c, parts=("consts", "sw", "qsks", "split"), njobs=None):
    g = c.geo
    nc = c.nc
    tload = []
    tload.append(P.dma("S", c.identf_s, c.identf[:, :]))
    tload.append(P.dma("S", c.identb_s, c.identb[:, :]))
    tload.append(P.dma("S", c.seld_s, c.seld[:, :, :]))
    tload.append(P.dma("S", c.sel64_s, c.sel64[:, :]))
    tload.append(P.dma("S", c.ones_s, c.onesc[:, :]))
    tload.append(P.dma("S", c.mstat_s, c.mstat.rearrange("a p c -> p a c")))
    tload.append(P.dma("S", c.lng_s, c.ln_g[:, :, :]))
    tload.append(P.dma("S", c.lnb_s, c.ln_b[:, :, :]))
    tload.append(P.dma("S", c.b2_s, c.b2[:, :, :]))
    tload.append(P.dma("S", c.b1_s, c.b1p[:, :, :, :]))
    tld = tload[-1]
    P.emit("V", lambda e: e.memset(c.eps_s, 1e-5))
    P.emit("V", lambda e: e.memset(c.barsc[:, 3:4], -0.5))
    P.emit("V", lambda e: e.memset(c.barsc[:, 4:5], 1.5))
    P.emit("V", lambda e: e.memset(c.x1col, 0.0))
    P.emit("V", lambda e: e.memset(c.o1col, 0.0))
    P.emit("V", lambda e: e.memset(c.sums2, 0.0))
    P.emit("V", lambda e: e.memset(c.sums2b, 0.0))
    P.emit("V", lambda e: e.memset(c.cols, 0.0))
    tmem = P.emit("V", lambda e: e.memset(c.dtmp, 0.0), mark=True)

    tsw = tld
    tq = [tld]
    if "sw" in parts:
        t0 = P.dma("S", c.sums2[0:1, 0:2], c.sw.rearrange("(o a) -> o a", o=1),
                   deps=[tmem])
        def swchain(e):
            return e.reduce_max(c.sums2[0:1, 4:5], c.sums2[0:1, 0:2], axis=AX.X)
        P.emit("V", swchain, deps=[t0, tmem])
        P.vdrain()
        tv = P.emit("V", lambda e: e.tensor_scalar_mul(c.sums2[0:1, 5:6],
                                                       c.sums2[0:1, 4:5], -1.0),
                    mark=True)
        ta = P.emit("A", lambda e: e.activation(c.sums2[0:1, 2:4], c.sums2[0:1, 0:2],
                                                AF.Exp, bias=c.sums2[0:1, 5:6],
                                                accum_out=c.sums2[0:1, 6:7]),
                    deps=[tv], mark=True)
        P.emit("V", lambda e: e.reciprocal(c.sums2[0:1, 7:8], c.sums2[0:1, 6:7]), deps=[ta])
        P.vdrain()
        tv = P.emit("V", lambda e: e.tensor_scalar(c.sums2[0:1, 2:4], c.sums2[0:1, 2:4],
                                                   c.sums2[0:1, 7:8], None, op0=OP.mult),
                    mark=True)
        ta = P.emit("A", lambda e: e.copy(c.statb[0:1, 0, 0:2], c.sums2[0:1, 2:4]),
                    deps=[tv], mark=True)
        tv = P.emit("V", lambda e: e.tensor_tensor(c.statb[0:1, 0, 2:4], c.sums2[0:1, 2:4],
                                                   c.statb[0:1, 0, 0:2], op=OP.subtract),
                    deps=[ta], mark=True)
        tT = P.emit("T", lambda e: e.matmul(c.PS_C[:, 0:2], c.ones_s[0:1, 0:128],
                                            c.statb[0:1, 0, 0:2], start=True, stop=False),
                    deps=[tv, tld])
        tT = P.emit("T", lambda e: e.matmul(c.PS_C[:, 0:2], c.ones_s[0:1, 0:128],
                                            c.statb[0:1, 0, 2:4], start=False, stop=True),
                    mark=True)
        tsw = P.emit("A", lambda e: e.copy(c.sw_s, c.PS_C[:, 0:2]), deps=[tT], mark=True)
        tq = [tsw]

    if "qsks" in parts:
        for li in range(2):
            for (src, dst, name) in ((c.qsc, c.qs_s, "q"), (c.ksc, c.ks_s, "k")):
                t0 = P.dma("S", c.sums2[0:1, 0:64], src[li:li+1, :],
                           deps=[tmem, P.tok("V")])
                ta = P.emit("A", lambda e: e.copy(c.brow[0:1, 0:64], c.sums2[0:1, 0:64]),
                            deps=[t0, tsw], mark=True)
                tv = P.emit("V", lambda e: e.tensor_tensor(
                    c.brow[0:1, 64:128], c.sums2[0:1, 0:64], c.brow[0:1, 0:64],
                    op=OP.subtract), deps=[ta], mark=True)
                P.emit("T", lambda e: e.matmul(c.PS_C[:, 0:64], c.ones_s[0:1, 0:128],
                                               c.brow[0:1, 0:64], start=True, stop=False),
                       deps=[tv])
                tT = P.emit("T", lambda e: e.matmul(c.PS_C[:, 0:64], c.ones_s[0:1, 0:128],
                                                    c.brow[0:1, 64:128], start=False,
                                                    stop=True), mark=True)
                tq.append(P.emit("A", lambda e, dst=dst, li=li: e.copy(
                    dst[:, li, 0, :], c.PS_C[:, 0:64]), deps=[tT], mark=True))
                tq.append(P.emit("A", lambda e, dst=dst, li=li: e.copy(
                    dst[:, li, 1, 0:32], c.PS_C[:, 32:64]), mark=True))
                tq.append(P.emit("A", lambda e, dst=dst, li=li: e.copy(
                    dst[:, li, 1, 32:64], c.PS_C[:, 0:32]), mark=True))

    P.emit("V", lambda e: e.memset(c.sums2, 0.0), deps=[tq[-1], P.tok("A")])
    tz = P.emit("V", lambda e: e.memset(c.sums2b, 0.0), mark=True)

    ws_f, ws_hi, ws_lo = [], [], []
    zb = c._zb
    for b in range(2):
        f, _ = _carve(nc, zb[0], zb[1], f"ws_f{b}", [128, 4096], F32, b * 32768)
        h, _ = _carve(nc, zb[0], zb[1], f"ws_hi{b}", [128, 4096], BF16, b * 32768 + 16384)
        l, _ = _carve(nc, zb[0], zb[1], f"ws_lo{b}", [128, 4096], BF16, b * 32768 + 24576)
        ws_f.append(f); ws_hi.append(h); ws_lo.append(l)

    jobs = []
    def add_jobs(src, hdst, rtiles, C):
        for r in range(rtiles):
            jobs.append((src, hdst, r, C))

    add_jobs(lambda r: c.WinT[128*r:128*(r+1), :],
             lambda r, dt: c.hWinT[:, 128*r:128*(r+1), dt, :]
             .rearrange("ec p col -> p ec col"), 8, 1024)
    add_jobs(lambda r: c.WattnT[128*r:128*(r+1), :],
             lambda r, dt: c.hWattnT[128*r:128*(r+1), dt, :], 8, 656)
    add_jobs(lambda r: c.WoutT[128*r:128*(r+1), :],
             lambda r, dt: c.hWoutT[128*r:128*(r+1), dt, :], 8, 1024)
    for li in range(2):
        add_jobs(lambda r, li=li: c.wqkv[li, 128*r:128*(r+1), :],
                 lambda r, dt, li=li: c.hwqkv[li, 128*r:128*(r+1), dt, :], 8, 1536)
        add_jobs(lambda r, li=li: c.woh[li].rearrange("h s e -> (h s) e")[128*r:128*(r+1), :],
                 lambda r, dt, li=li: c.hwoh[li].rearrange("h s d e -> (h s) d e")[128*r:128*(r+1), dt, :],
                 4, 1024)
        for half in range(2):
            add_jobs(lambda r, li=li, half=half: c.w1p[li, 128*r:128*(r+1),
                                                       2816*half:2816*(half+1)],
                     lambda r, dt, li=li, half=half: c.hw1p[li, 22*half:22*half+22,
                                                            128*r:128*(r+1), dt, :]
                     .rearrange("ch p col -> p ch col"), 8, 2816)
        add_jobs(lambda r, li=li: c.w2p[li, 128*r:128*(r+1), :],
                 lambda r, dt, li=li: c.hw2p[li, 128*r:128*(r+1), dt, :], 22, 1024)

    if "split" not in parts:
        jobs = []
    elif njobs is not None:
        jobs = jobs[:njobs]
    free_tok = [tz, tz]
    st_tok = [None, None]
    for i, (src, hdst, r, C) in enumerate(jobs):
        b = i % 2
        tin = P.dma("S", ws_f[b][:, 0:C], src(r), deps=[free_tok[b]])
        ta = P.emit("A", lambda e, b=b, C=C: e.copy(ws_hi[b][:, 0:C], ws_f[b][:, 0:C]),
                    deps=[tin, st_tok[b]], mark=True)
        tv = P.emit("V", lambda e, b=b, C=C: e.tensor_tensor(
            ws_lo[b][:, 0:C], ws_f[b][:, 0:C], ws_hi[b][:, 0:C], op=OP.subtract),
            deps=[ta], mark=True)
        t1 = P.dma("G", hdst(r, 0), ws_hi[b][:, 0:C], deps=[ta])
        t2 = P.dma("G", hdst(r, 1), ws_lo[b][:, 0:C], deps=[tv])
        free_tok[b] = tv
        st_tok[b] = t2
    P.barrier(with_dma=True)


def _v3(ps_slice, a=8, b=128):
    return ps_slice.rearrange("p (a b) -> p a b", a=a)


def emit_ph0(P, c, k):
    g = c.geo
    toff = g.t0_off(k)
    ntiles = 6 if k >= 0 else 5
    if globals().get("PH0_TILES"):
        ntiles = min(ntiles, globals()["PH0_TILES"])
    inv = 1.0 / 1024.0
    zx_free = c.free.get("zx")
    psA_free = c.free.get("PS_A")
    tT_prev = None
    for j in range(ntiles):
        trel = -128 + 128 * j
        rows = toff + trel
        tin = P.dma("S", c.zx, c.x[rows:rows + 128, :], deps=[zx_free], stream="x")
        tv1 = P.emit("V", lambda e: e.reduce_sum(c.cols[:, 0:1], c.zx, axis=AX.X),
                     deps=[tin], mark=True)
        P.emit("V", lambda e: e.tensor_tensor(c.zxn, c.zx, c.zx, op=OP.mult),
               deps=[tin, tT_prev])
        P.emit("V", lambda e: e.reduce_sum(c.cols[:, 1:2], c.zxn, axis=AX.X))
        P.emit("V", lambda e: e.tensor_scalar_mul(c.cols[:, 2:3], c.cols[:, 0:1], inv))
        P.emit("V", lambda e: e.tensor_scalar_mul(c.cols[:, 3:4], c.cols[:, 1:2], inv))
        P.emit("V", lambda e: e.tensor_tensor(c.cols[:, 4:5], c.cols[:, 2:3],
                                              c.cols[:, 2:3], op=OP.mult))
        P.vdrain()
        tv2 = P.emit("V", lambda e: e.tensor_tensor(c.cols[:, 3:4], c.cols[:, 3:4],
                                                    c.cols[:, 4:5], op=OP.subtract),
                     mark=True)
        ta2 = P.emit("A", lambda e: e.activation(c.cols[:, 5:6], c.cols[:, 3:4],
                                                 AF.Sqrt, bias=c.eps_s),
                     deps=[tv2], mark=True)
        P.emit("V", lambda e: e.reciprocal(c.cols[:, 6:7], c.cols[:, 5:6]), deps=[ta2])
        P.emit("V", lambda e: e.tensor_scalar(c.cols[:, 7:8], c.cols[:, 3:4],
                                              c.eps_s, None, op0=OP.add))
        P.emit("V", lambda e: e.tensor_tensor(c.cols[:, 8:9], c.cols[:, 6:7],
                                              c.cols[:, 6:7], op=OP.mult))
        P.vdrain()
        P.emit("V", lambda e: e.tensor_tensor(c.cols[:, 9:10], c.cols[:, 7:8],
                                              c.cols[:, 8:9], op=OP.mult))
        P.vdrain()
        P.emit("V", lambda e: e.tensor_scalar(c.cols[:, 10:11], c.cols[:, 9:10],
                                              c.barsc[:, 3:4], c.barsc[:, 4:5],
                                              op0=OP.mult, op1=OP.add))
        P.vdrain()
        P.emit("V", lambda e: e.tensor_tensor(c.cols[:, 6:7], c.cols[:, 6:7],
                                              c.cols[:, 10:11], op=OP.mult))
        P.vdrain()
        tv4 = P.emit("V", lambda e: e.tensor_scalar(c.zxn, c.zx, c.cols[:, 2:3],
                                                    c.cols[:, 6:7], op0=OP.subtract,
                                                    op1=OP.mult),
                     mark=True)
        for d in range(8):
            P.emit("T", lambda e, d=d: e.transpose(c.PS_A[:, 128 * d:128 * d + 128],
                                                   c.zxn[:, 128 * d:128 * d + 128],
                                                   c.identf_s),
                   deps=[tv4, psA_free] if d == 0 else ())
        tT = P.emit("T", lambda e: e.nop(), mark=True)
        tT_prev = tT
        col = 128 * j
        psv = _v3(c.PS_A[:, 0:1024])
        ta = P.emit("A", lambda e, col=col: e.activation(
            c.xn_hi[:, :, col:col + 128], psv, AF.Copy), deps=[tT], mark=True)
        scr = _v3(c.dtmp[:, 0:1024])
        P.emit("V", lambda e, col=col, scr=scr: e.tensor_tensor(
            scr, psv, c.xn_hi[:, :, col:col + 128], op=OP.subtract),
            deps=[ta])
        tv = P.emit("V", lambda e, col=col, scr=scr: e.tensor_copy(
            c.xn_lo[:, :, col:col + 128], scr), mark=True)
        tf = None
        if 0 <= trel < 512 and not globals().get("NO_XNF"):
            tf = P.emit("V", lambda e, trel=trel: e.tensor_copy(
                c.xn_f[:, :, trel:trel + 128], psv), mark=True)
        psA_free = tv if tf is None else tf
        zx_free = tv4
    c.free["zx"] = zx_free
    c.free["PS_A"] = psA_free
    c.free["xnA"] = P.tok("A")
    c.free["xnV"] = P.tok("V")


def emit_ph1(P, c, k):
    g = c.geo
    toff = g.t0_off(k)
    tacc = P.dma("S", c.acc,
                 c.x[toff:toff + 512, :].rearrange("(a p) e -> p a e", p=128),
                 deps=[c.free.get("acc")], stream="acc")
    psA_free = c.free.get("PS_A")
    psB_free = c.free.get("PS_B")
    psC_free = c.free.get("PS_C")
    mwb = 0
    mw_free = [c.free.get("mw0"), c.free.get("mw1")]
    a_ev = None
    for e8 in range(8):
        tw = P.dma("S", c.mw[:, mwb].rearrange("p dc dt col -> p dc (dt col)"),
                   c.hWinT[e8].rearrange("(dc p) dt col -> p dc (dt col)", p=128),
                   deps=[mw_free[mwb]], stream=f"mwa{mwb}")
        mwv = c.mw[:, mwb]
        if globals().get("A_ILV"):
            for d in range(8):
                lh = mwv[:, d, 0, :]
                ll = mwv[:, d, 1, :]
                first = (d == 0)
                last = (d == 7)
                dep0 = [tw, psA_free, a_ev, c.free.get("xnA"), c.free.get("xnV")] if first else ()
                mm3(P, c.PS_A[:, 0:512], lh, ll, c.xn_hi[:, d, 0:512],
                    c.xn_lo[:, d, 0:512], first, last, deps=dep0)
                mm3(P, c.PS_A[:, 512:768], lh, ll, c.xn_hi[:, d, 512:768],
                    c.xn_lo[:, d, 512:768], first, last)
        else:
            for half, (h0, h1) in enumerate(((0, 512), (512, 768))):
                for d in range(8):
                    lh = mwv[:, d, 0, :]
                    ll = mwv[:, d, 1, :]
                    first = (d == 0)
                    last = (d == 7)
                    dep0 = ([tw, psA_free, a_ev, c.free.get("xnA"),
                             c.free.get("xnV")] if first and half == 0 else ())
                    mm3(P, c.PS_A[:, h0:h1], lh, ll, c.xn_hi[:, d, h0:h1],
                        c.xn_lo[:, d, h0:h1], first, last, deps=dep0)
        tT = P.emit("T", lambda e: e.nop(), mark=True)
        mw_free[mwb] = tT
        mwb ^= 1
        a_ev = P.emit("A", lambda e, e8=e8: e.activation(
            c.a_f[:, e8, :], c.PS_A[:, 0:768], AF.Copy), deps=[tT], mark=True)
    psA_free = a_ev

    P.emit("V", lambda e: e.memset(c.w_tm, 0.0))
    rj_free = [None, None]
    w_done = []
    for t in range(4):
        pm_last = None
        for e8 in range(8):
            b = (t * 8 + e8) % 2
            mwfl = c.mw[:, mwb].rearrange("p a b2 c2 -> p (a b2 c2)")
            tw = P.dma("S", mwfl[:, 0:1312].rearrange("p (dt col) -> p dt col", dt=2),
                       c.hWattnT[128 * e8:128 * e8 + 128, :, :],
                       deps=[mw_free[mwb]], stream=f"mwp{mwb}")
            asl = c.a_f[:, e8, 128 + 128 * t: 256 + 128 * t]
            ta = P.emit("A", lambda e, asl=asl, b=b: e.activation(
                c.rj_hi[:, b, :], asl, AF.Relu), deps=[a_ev, rj_free[b]], mark=True)
            tv = P.emit("V", lambda e, asl=asl, b=b: e.scalar_tensor_tensor(
                c.rj_lo[:, b, :], asl, 0.0, c.rj_hi[:, b, :],
                op0=OP.max, op1=OP.subtract), deps=[ta], mark=True)
            first = (e8 == 0)
            last = (e8 == 7)
            dep0 = [tw, tv, psB_free] if first else [tw, tv]
            mwv = mwfl[:, 0:1312].rearrange("p (dt col) -> p dt col", dt=2)
            mm3(P, c.PS_B[:, 0:512], c.rj_hi[:, b, :], c.rj_lo[:, b, :],
                mwv[:, 0, 0:512], mwv[:, 1, 0:512], first, last, deps=dep0)
            mm3(P, c.PS_B[:, 512:656], c.rj_hi[:, b, :], c.rj_lo[:, b, :],
                mwv[:, 0, 512:656], mwv[:, 1, 512:656], first, last)
            pm_last = P.emit("T", lambda e: e.nop(), mark=True)
            mw_free[mwb] = pm_last
            rj_free[b] = pm_last
            mwb ^= 1
        segs = [(0, 0, 21, 0), (328, 192, 41, 1)]
        tv_end = None
        for (p0, c0, cs, si) in segs:
            pv = c.PS_B[:, p0:p0 + 8 * cs].rearrange("p (h c) -> p h c", h=8)
            tm = P.emit("V", lambda e, pv=pv: e.reduce_max(
                c.cols[:, 0:8], pv, axis=AX.X), deps=[pm_last], mark=True)
            tn = P.emit("V", lambda e: e.tensor_scalar_mul(
                c.cols[:, 8:16], c.cols[:, 0:8], -1.0), mark=True)
            exps = []
            for h in range(8):
                exps.append(P.emit("A", lambda e, pv=pv, h=h, c0=c0, cs=cs, t=t:
                                   e.activation(c.w_tm[:, t, c0 + h * cs: c0 + (h + 1) * cs],
                                                pv[:, h, :], AF.Exp,
                                                bias=c.cols[:, 8 + h:9 + h]),
                                   deps=[tn], mark=True))
            wv = c.w_tm[:, t, c0:c0 + 8 * cs].rearrange("p (h c) -> p h c", h=8)
            ts_ = P.emit("V", lambda e, wv=wv: e.reduce_sum(
                c.cols[:, 0:8], wv, axis=AX.X), deps=[exps[-1]], mark=True)
            P.emit("V", lambda e: e.reciprocal(c.cols[:, 8:16], c.cols[:, 0:8]))
            P.vdrain()
            P.emit("V", lambda e, si=si: e.tensor_scalar(
                c.cols[:, 8:16], c.cols[:, 8:16], c.sw_s[:, si:si + 1], None,
                op0=OP.mult))
            P.vdrain()
            tv_end = P.emit("V", lambda e, wv=wv, cs=cs: e.tensor_tensor(
                wv, wv, c.cols[:, 8:16].rearrange("p (h o) -> p h o", o=1)
                .broadcast_to([128, 8, cs]), op=OP.mult), mark=True)
        w_done.append(tv_end)
        psB_free = tv_end
    if globals().get("PH1_TRUNC"):
        c.free["PS_A"] = psA_free
        c.free["PS_B"] = psB_free
        c.free["mw0"] = mw_free[0]
        c.free["mw1"] = mw_free[1]
        c.free["acc_ev"] = tacc
        return

    dyn_last = None
    tr_prev = None
    for si, (c0, cs, pad) in enumerate([(0, 21, 10), (192, 41, 20)]):
        for gi in range(2):
            tms = P.emit("V", lambda e: e.memset(c.wfm2, 0.0),
                         deps=[dyn_last, P.tok("T")], mark=True)
            ev_done = [tms]
            for h4 in range(4):
                h = 4 * gi + h4
                for t in range(4):
                    seg = c.w_tm[:, t, c0 + h * cs: c0 + h * cs + 41]
                    segd = seg.rearrange("p (o c) -> p o c", o=1).broadcast_to([128, 2, 41])
                    dup = c.dtmp[:, 0:128]
                    dupv = dup.rearrange("p (o cc) -> p o cc", o=2)[:, :, 0:41]
                    tdup = P.emit("V", lambda e, segd=segd, dupv=dupv: e.tensor_copy(
                        dupv, segd), deps=[w_done[t], dyn_last, tr_prev], mark=True)
                    tr_prev = P.emit("T", lambda e, dup=dup, t=t: e.transpose(
                        c.PS_C[:, 128 * t:128 * t + 128], dup, c.identf_s),
                        deps=[tdup, psC_free], mark=True)
                tT = P.emit("T", lambda e: e.nop(), mark=True)
                t1 = P.emit("A", lambda e, h4=h4: e.activation(
                    c.wfm2[0:41, h4, :], c.PS_C[0:41, :], AF.Copy), deps=[tT], mark=True)
                t2 = P.emit("A", lambda e, h4=h4: e.activation(
                    c.wfm2[64:105, h4, :], c.PS_C[64:105, :], AF.Copy), mark=True)
                P.emit("V", lambda e, h4=h4: e.tensor_tensor(
                    c.dtmp[64:105, 512:1024], c.PS_C[64:105, :],
                    c.wfm2[64:105, h4, :], op=OP.subtract), deps=[t2])
                t3 = P.emit("V", lambda e, h4=h4: e.tensor_copy(
                    c.wfm2[64:105, h4, :], c.dtmp[64:105, 512:1024]), mark=True)
                psC_free = t3
                ev_done += [t1, t3]
            wflat = c.wfm2.rearrange("p a b -> p (a b)")
            mulps = [None, None]
            for j in range(cs):
                dlt = j - pad
                for half in range(2):
                    tb = P.emit("T", lambda e, j=j, half=half: (
                        e.matmul(c.PS_A[:, 1024 * half: 1024 * half + 512],
                                 c.seld_s[:, j, :], wflat[:, 1024 * half: 1024 * half + 512],
                                 start=True, stop=True)),
                        deps=(ev_done + [psA_free]) if (j == 0 and half == 0) else [mulps[half]])
                    tb = P.emit("T", lambda e, j=j, half=half: (
                        e.matmul(c.PS_A[:, 1024 * half + 512: 1024 * half + 1024],
                                 c.seld_s[:, j, :], wflat[:, 1024 * half + 512: 1024 * half + 1024],
                                 start=True, stop=True)), mark=True)
                    ach = slice(4 * gi + 2 * half, 4 * gi + 2 * half + 2)
                    av = c.a_f[:, ach, 128 + dlt: 640 + dlt]
                    psv = c.PS_A[:, 1024 * half: 1024 * half + 1024].rearrange(
                        "p (a b) -> p a b", a=2)
                    tmpv = c.dtmp.rearrange("p (a b) -> p a b", a=2)
                    tm = P.emit("V", lambda e, av=av, psv=psv, tmpv=tmpv: e.tensor_tensor(
                        tmpv, av, psv, op=OP.mult), deps=[tb], mark=True)
                    mulps[half] = tm
                    dv = c.dyn_f[:, ach, :]
                    final = (si == 1 and j == cs - 1)
                    if si == 0 and j == 0:
                        dyn_last = P.emit("V", lambda e, dv=dv, tmpv=tmpv: e.tensor_copy(
                            dv, tmpv), mark=True)
                    else:
                        dyn_last = P.emit("V", lambda e, dv=dv, tmpv=tmpv: e.tensor_tensor(
                            dv, dv, tmpv, op=OP.add), mark=True)
    psA_free = dyn_last

    dj_free = [None, None]
    for t in range(4):
        for d in range(8):
            b = (t * 8 + d) % 2
            mwfl = c.mw[:, mwb].rearrange("p a b2 c2 -> p (a b2 c2)")
            tw = P.dma("S", mwfl.rearrange("p (dt col) -> p dt col", dt=2),
                       c.hWoutT[128 * d:128 * d + 128, :, :],
                       deps=[mw_free[mwb]], stream=f"mwo{mwb}")
            dsl = c.dyn_f[:, d, 128 * t:128 * t + 128]
            ta = P.emit("A", lambda e, dsl=dsl, b=b: e.activation(
                c.dj_hi[:, b, :], dsl, AF.Copy), deps=[dyn_last, dj_free[b]], mark=True)
            tv = P.emit("V", lambda e, dsl=dsl, b=b: e.tensor_tensor(
                c.dj_lo[:, b, :], dsl, c.dj_hi[:, b, :], op=OP.subtract),
                deps=[ta], mark=True)
            first = (d == 0)
            last = (d == 7)
            mwv = mwfl.rearrange("p (dt col) -> p dt col", dt=2)
            dep0 = [tw, tv, psB_free] if first else [tw, tv]
            mm3(P, c.PS_B[:, 0:512], c.dj_hi[:, b, :], c.dj_lo[:, b, :],
                mwv[:, 0, 0:512], mwv[:, 1, 0:512], first, last, deps=dep0)
            mm3(P, c.PS_B[:, 512:1024], c.dj_hi[:, b, :], c.dj_lo[:, b, :],
                mwv[:, 0, 512:1024], mwv[:, 1, 512:1024], first, last)
            tT = P.emit("T", lambda e: e.nop(), mark=True)
            mw_free[mwb] = tT
            dj_free[b] = tT
            mwb ^= 1
        psB_free = P.emit("V", lambda e, t=t: e.tensor_tensor(
            c.acc[:, t, :], c.acc[:, t, :], c.PS_B, op=OP.add),
            deps=[P.tok("T"), tacc], mark=True)
    c.free["PS_A"] = psA_free
    c.free["PS_B"] = psB_free
    c.free["PS_C"] = psC_free
    c.free["mw0"] = mw_free[0]
    c.free["mw1"] = mw_free[1]
    c.free["acc_ev"] = psB_free


def emit_ffn(P, c, k, l, mode):
    g = c.geo
    toff = g.t0_off(k)
    slot = k + g.warm
    full = (mode in ("full", "x1"))

    rdep = [P.tok("A"), P.tok("T")]
    if l == 0:
        P.emit("V", lambda e: e.tensor_copy(
            c.o1_f[:, :, 0:1].rearrange("p a o -> p (a o)"), c.o1col), deps=rdep)
        if full:
            P.emit("V", lambda e: e.tensor_copy(
                c.x1_f[:, :, 0:1].rearrange("p a o -> p (a o)"), c.x1col[:, 0, :]),
                deps=rdep)
    else:
        if full:
            P.emit("V", lambda e: e.tensor_copy(
                c.x1_f[:, :, 0:1].rearrange("p a o -> p (a o)"), c.x1col[:, 1, :]),
                deps=rdep)

    if l == 0:
        def IN(dt, d, t, sh):
            buf = c.xn_hi if dt == 0 else c.xn_lo
            c0 = 128 + 128 * t - (1 if sh else 0)
            return buf[:, d, c0:c0 + 128]
        in_ready = [c.free.get("xnA"), c.free.get("xnV")]
    else:
        ta = P.emit("A", lambda e: e.activation(c.o1s_hi, c.o1_f, AF.Copy),
                    deps=[c.free.get("o1_done"), P.tok("T")], mark=True)
        tv = P.emit("V", lambda e: e.tensor_tensor(c.o1s_lo, c.o1_f, c.o1s_hi,
                                                   op=OP.subtract), deps=[ta], mark=True)
        def IN(dt, d, t, sh):
            buf = c.o1s_hi if dt == 0 else c.o1s_lo
            c0 = 128 * t + (0 if sh else 1)
            return buf[:, d, c0:c0 + 128]
        in_ready = [ta, tv]

    qb = 0
    qw_free = [c.free.get("qkvw0"), c.free.get("qkvw1")]
    ev_prev = []
    psA_free = c.free.get("PS_A")
    psB_free = c.free.get("PS_B")
    kcur_sp = [None, None]
    vcur_sp = [None, None]
    for gi, grp in enumerate(("q", "k", "v")):
        if grp == "q" and not full:
            continue
        for d in range(8):
            tw = P.dma("S", c.qkvw[:, qb],
                       c.hwqkv[l, 128 * d:128 * d + 128, :, 512 * gi:512 * gi + 512],
                       deps=[qw_free[qb]], stream=f"qkv{qb}")
            for t in range(4):
                first = (d == 0)
                last = (d == 7)
                dep0 = ([tw, psA_free] + ev_prev + in_ready) if first else [tw]
                mm3(P, c.PS_A[:, 512 * t:512 * t + 512],
                    IN(0, d, t, d >= 4), IN(1, d, t, d >= 4),
                    c.qkvw[:, qb, 0, :], c.qkvw[:, qb, 1, :],
                    first, last, deps=dep0)
            tT = P.emit("T", lambda e: e.nop(), mark=True)
            qw_free[qb] = tT
            qb ^= 1
        tql = P.tok("T")
        ev_prev = []
        if grp in ("q", "k"):
            isq = (grp == "q")
            TA, TB = (0, 1) if isq else (2, 3)
            scv = c.qs_s if isq else c.ks_s
            dsth = c.qfm_hi if isq else c.kcur_hi
            dstl = c.qfm_lo if isq else c.kcur_lo
            dst_dep = [] if isq else [c.free.get(f"kcur{l}")]
            tT = None
            for t in range(4):
                pst = c.PS_A[:, 512 * t:512 * t + 512]
                ta0 = P.emit("A", lambda e, pst=pst: e.activation(
                    c.qscr, pst, AF.Copy), deps=[tql, tT], mark=True)
                ev_prev.append(ta0)
                P.emit("V", lambda e: e.tensor_tensor(c.qscr2, c.qscr, c.qscr,
                                                      op=OP.mult),
                       deps=[ta0], mark=False)
                tv1 = P.emit("V", lambda e: e.tensor_reduce(
                    c.cols[:, 0:8], _v3(c.qscr2, 8, 64), axis=AX.X, op=OP.add),
                    mark=True)
                tvg = P.emit("V", lambda e: e.tensor_scalar_max(
                    c.cols[:, 0:8], c.cols[:, 0:8], 1e-24), mark=True)
                ta2 = P.emit("A", lambda e: e.activation(
                    c.cols[:, 8:16], c.cols[:, 0:8], AF.Sqrt), deps=[tvg],
                    mark=True)
                P.emit("V", lambda e: e.reciprocal(c.cols[:, 8:16], c.cols[:, 8:16]),
                       deps=[ta2])
                P.vdrain()
                P.emit("V", lambda e: e.tensor_tensor(
                    c.qscr2[:, 0:8], c.cols[:, 8:16], c.cols[:, 8:16], op=OP.mult))
                P.vdrain()
                P.emit("V", lambda e: e.tensor_tensor(
                    c.qscr2[:, 0:8], c.cols[:, 0:8], c.qscr2[:, 0:8], op=OP.mult))
                P.vdrain()
                P.emit("V", lambda e: e.tensor_scalar(
                    c.qscr2[:, 0:8], c.qscr2[:, 0:8], c.barsc[:, 3:4],
                    c.barsc[:, 4:5], op0=OP.mult, op1=OP.add))
                P.vdrain()
                P.emit("V", lambda e: e.tensor_tensor(
                    c.cols[:, 8:16], c.cols[:, 8:16], c.qscr2[:, 0:8], op=OP.mult))
                P.vdrain()
                tv4 = P.emit("V", lambda e: e.tensor_tensor(
                    _v3(c.qscr2, 8, 64), _v3(c.qscr, 8, 64),
                    c.cols[:, 8:16].rearrange("p (h o) -> p h o", o=1)
                    .broadcast_to([128, 8, 64]), op=OP.mult),
                    deps=[ta0], mark=True)
                P.emit("V", lambda e, t=t, TA=TA, scv=scv: e.tensor_tensor(
                    c.tabe[:, 0, :], c.tabs[:, t, TA, :], scv[:, l, 0, :],
                    op=OP.mult), deps=[c.free.get("tabs_tok")])
                tv6 = P.emit("V", lambda e, t=t, TB=TB, scv=scv: e.tensor_tensor(
                    c.tabe[:, 1, :], c.tabs[:, t, TB, :], scv[:, l, 1, :],
                    op=OP.mult), mark=True)
                qv = _v3(c.qscr, 8, 64)
                qnv = _v3(c.qscr2, 8, 64)
                t0v = c.tabe[:, 0:1, :].broadcast_to([128, 8, 64])
                P.emit("V", lambda e, qv=qv, qnv=qnv, t0v=t0v: e.tensor_tensor(
                    qv, qnv, t0v, op=OP.mult), deps=[tv4, tv6])
                dt1 = _v3(c.dtmp[:, 0:256], 8, 32)
                dt2 = _v3(c.dtmp[:, 256:512], 8, 32)
                b1v = c.tabe[:, 1:2, 0:32].broadcast_to([128, 8, 32])
                b2v = c.tabe[:, 1:2, 32:64].broadcast_to([128, 8, 32])
                P.emit("V", lambda e, qnv=qnv, dt1=dt1, b1v=b1v: e.tensor_tensor(
                    dt1, qnv[:, :, 32:64], b1v, op=OP.mult))
                P.emit("V", lambda e, qv=qv, dt1=dt1: e.tensor_tensor(
                    qv[:, :, 0:32], qv[:, :, 0:32], dt1, op=OP.add))
                P.emit("V", lambda e, qnv=qnv, dt2=dt2, b2v=b2v: e.tensor_tensor(
                    dt2, qnv[:, :, 0:32], b2v, op=OP.mult))
                tv7 = P.emit("V", lambda e, qv=qv, dt2=dt2: e.tensor_tensor(
                    qv[:, :, 32:64], qv[:, :, 32:64], dt2, op=OP.add), mark=True)
                for c4 in range(4):
                    P.emit("T", lambda e, c4=c4: e.transpose(
                        c.PS_B[:, 128 * c4:128 * c4 + 128],
                        c.qscr[:, 128 * c4:128 * c4 + 128], c.identf_s),
                        deps=[tv7, psB_free] if c4 == 0 else ())
                tT = P.emit("T", lambda e: e.nop(), mark=True)
                psv = _v3(c.PS_B[:, 0:512], 4, 128)
                ta = P.emit("A", lambda e, t=t, dsth=dsth, psv=psv: e.activation(
                    dsth[:, :, 128 * t:128 * t + 128], psv, AF.Copy),
                    deps=[tT] + dst_dep, mark=True)
                scr = _v3(c.dtmp[:, 512:1024], 4, 128)
                P.emit("V", lambda e, t=t, dsth=dsth, psv=psv, scr=scr:
                       e.tensor_tensor(scr, psv,
                                       dsth[:, :, 128 * t:128 * t + 128],
                                       op=OP.subtract), deps=[ta])
                tv = P.emit("V", lambda e, t=t, dstl=dstl, scr=scr: e.tensor_copy(
                    dstl[:, :, 128 * t:128 * t + 128], scr), mark=True)
                psB_free = tv
            if grp == "k":
                kcur_sp[0] = P.dma("G", c.kspill[l, slot, 0].rearrange(
                    "(cc p) t -> p cc t", p=128), c.kcur_hi, deps=[P.tok("A")],
                    stream="ksp")
                kcur_sp[1] = P.dma("G", c.kspill[l, slot, 1].rearrange(
                    "(cc p) t -> p cc t", p=128), c.kcur_lo, deps=[P.tok("V")],
                    stream="ksp")
        else:
            vones_h = c.vcur_hi.rearrange("p t (h u) -> p t h u", u=65)[:, :, :, 64]
            vones_l = c.vcur_lo.rearrange("p t (h u) -> p t h u", u=65)[:, :, :, 64]
            tm1 = P.emit("V", lambda e: e.memset(vones_h, 1.0),
                         deps=[c.free.get(f"vcur{l}")], mark=True)
            tm2 = P.emit("V", lambda e: e.memset(vones_l, 0.0), mark=True)
            va_h = c.vcur_hi.rearrange("p t (h u) -> p t h u", u=65)[:, :, :, 0:64]
            va_l = c.vcur_lo.rearrange("p t (h u) -> p t h u", u=65)[:, :, :, 0:64]
            for t in range(4):
                psv = _v3(c.PS_A[:, 512 * t:512 * t + 512], 8, 64)
                ta = P.emit("A", lambda e, t=t, psv=psv: e.activation(
                    va_h[:, t], psv, AF.Copy), deps=[tql, tm1], mark=True)
                ev_prev.append(ta)
                scr = _v3(c.dtmp[:, 512:1024], 8, 64)
                P.emit("V", lambda e, t=t, psv=psv, scr=scr: e.tensor_tensor(
                    scr, psv, va_h[:, t], op=OP.subtract), deps=[ta, tm2])
                tv = P.emit("V", lambda e, t=t, scr=scr: e.tensor_copy(
                    va_l[:, t], scr), mark=True)
            vcur_sp[0] = P.dma("G", c.vspill[l, slot, 0].rearrange(
                "(cc p) t -> p cc t", p=128), c.vcur_hi, deps=[P.tok("A")],
                stream="vsp")
            vcur_sp[1] = P.dma("G", c.vspill[l, slot, 1].rearrange(
                "(cc p) t -> p cc t", p=128), c.vcur_lo, deps=[P.tok("V")],
                stream="vsp")
    c.free[f"kcur{l}"] = kcur_sp[1]
    c.free[f"vcur{l}"] = vcur_sp[1]
    c.free[f"kspill_w{l}"] = [t for t in kcur_sp + vcur_sp if t is not None]
    c.free["PS_A"] = ev_prev[-1] if ev_prev else psA_free
    c.free["PS_B"] = psB_free
    c.free["qkvw0"] = qw_free[0]
    c.free["qkvw1"] = qw_free[1]
    if not full:
        return

    tkp = [P.dma("S", c.kprv_hi, c.kspill[l, slot - 1, 0].rearrange(
               "(cc p) t -> p cc t", p=128),
               deps=[c.free.get(f"kspill_r{l}"), P.tok("V"), P.tok("A")]
               + c.free.get(f"kspill_w{l}", []), stream="kvl"),
           P.dma("S", c.kprv_lo, c.kspill[l, slot - 1, 1].rearrange(
               "(cc p) t -> p cc t", p=128), stream="kvl"),
           P.dma("S", c.vprv_hi, c.vspill[l, slot - 1, 0].rearrange(
               "(cc p) t -> p cc t", p=128), stream="kvl2"),
           P.dma("S", c.vprv_lo, c.vspill[l, slot - 1, 1].rearrange(
               "(cc p) t -> p cc t", p=128), stream="kvl2")]
    tmv = P.dma("S", c.mvar_s, c.mvar[k + 2].rearrange("a p t -> p a t"), stream="mvar",
                deps=[c.free.get("mvar")])
    qkv_done_A = P.tok("A")
    qkv_done_V = P.tok("V")
    psA_free = c.free.get("PS_A")
    psB_free = c.free.get("PS_B")
    ops_free = [c.free.get("PS_C"), c.free.get("PS_D")]
    wow_free = [c.free.get("wow0"), c.free.get("wow1")]
    o_free = [None, None]
    wo_ps_free = None
    exp_done = None
    x1_last = None
    for h in range(8):
        hb = h % 2
        opsum = c.PS_C if hb == 0 else c.PS_D
        base = 64 * (h % 2)
        c4h = h // 2
        twl = P.dma("S", c.wow[:, hb, :, :], c.hwoh[l, h], stream=f"wow{hb}",
                    deps=[wow_free[hb], c.free.get("mw0"), c.free.get("mw1")])
        tpv = None
        pv_prev = c.free.get("pv_prev")
        for half in range(2):
            for ci in range(4):
                cg = 4 * half + ci
                kbh = c.kprv_hi if cg < 4 else c.kcur_hi
                kbl = c.kprv_lo if cg < 4 else c.kcur_lo
                tb = cg % 4
                bank = c.PS_A[:, 512 * ci:512 * ci + 512]
                mask = c.mvar_s[:, ci, :] if cg < 4 else c.mstat_s[:, ci, :]
                dep0 = ([tmv, tkp[-1], exp_done, psA_free, qkv_done_A, qkv_done_V]
                        if (ci == 0) else ())
                P.emit("T", lambda e, bank=bank, mask=mask: e.matmul(
                    bank, c.identb_s, mask, start=True, stop=False), deps=dep0)
                lh = kbh[base:base + 64, c4h, 128 * tb:128 * tb + 128]
                llo = kbl[base:base + 64, c4h, 128 * tb:128 * tb + 128]
                rh = c.qfm_hi[base:base + 64, c4h, :]
                rl = c.qfm_lo[base:base + 64, c4h, :]
                P.emit("T", lambda e, bank=bank, lh=lh, rh=rh: e.matmul(
                    bank, lh, rh, start=False, stop=False))
                P.emit("T", lambda e, bank=bank, lh=lh, rl=rl: e.matmul(
                    bank, lh, rl, start=False, stop=False))
                P.emit("T", lambda e, bank=bank, llo=llo, rh=rh: e.matmul(
                    bank, llo, rh, start=False, stop=True))
            tqk = P.emit("T", lambda e: e.nop(), mark=True)
            te1 = P.emit("A", lambda e: e.activation(c.P_hi, c.PS_A[:, 0:2048],
                                                     AF.Exp), deps=[tqk, pv_prev],
                         mark=True)
            tvs = []
            for qtr in range(4):
                te2 = P.emit("A", lambda e, qtr=qtr: e.activation(
                    c.P_f, c.PS_A[:, 512 * qtr:512 * qtr + 512], AF.Exp),
                    deps=[tvs[-1]] if tvs else (), mark=True)
                tvs.append(P.emit("V", lambda e, qtr=qtr: e.tensor_tensor(
                    c.P_lo[:, 512 * qtr:512 * qtr + 512], c.P_f,
                    c.P_hi[:, 512 * qtr:512 * qtr + 512], op=OP.subtract),
                    deps=[te2], mark=True))
            exp_done = tvs[-1]
            for ci in range(4):
                cg = 4 * half + ci
                vbh = c.vprv_hi if cg < 4 else c.vcur_hi
                vbl = c.vprv_lo if cg < 4 else c.vcur_lo
                tb = cg % 4
                lh = vbh[:, tb, 65 * h:65 * h + 65]
                llo = vbl[:, tb, 65 * h:65 * h + 65]
                rh = c.P_hi[:, 512 * ci:512 * ci + 512]
                rl = c.P_lo[:, 512 * ci:512 * ci + 512]
                first = (half == 0 and ci == 0)
                last = (half == 1 and ci == 3)
                dep0 = [exp_done, ops_free[hb]] if first else [exp_done]
                mm3(P, opsum[0:65, :], lh, llo, rh, rl, first, last, deps=dep0)
            tpv = P.emit("T", lambda e: e.nop(), mark=True)
            pv_prev = tpv
        c.free["pv_prev"] = tpv
        ta_s = P.emit("A", lambda e, opsum=opsum: e.activation(
            c.sums2[64:65, 0:512], opsum[64:65, :], AF.Copy), deps=[tpv], mark=True)
        tv_r = P.emit("V", lambda e: e.reciprocal(
            c.sums2[64:65, 512:1024], c.sums2[64:65, 0:512]), deps=[ta_s], mark=True)
        ta_rh = P.emit("A", lambda e: e.activation(
            c.sums2b[64:65, 0:512], c.sums2[64:65, 512:1024], AF.Copy),
            deps=[tv_r], mark=True)
        tv_rl = P.emit("V", lambda e: e.tensor_tensor(
            c.sums2b[64:65, 512:1024], c.sums2[64:65, 512:1024],
            c.sums2b[64:65, 0:512], op=OP.subtract), deps=[ta_rh], mark=True)
        P.emit("T", lambda e: e.matmul(
            c.PS_B[:, 0:512], c.sel64_s, c.sums2b[:, 0:512], start=True, stop=False),
            deps=[tv_rl, psB_free])
        tb_ = P.emit("T", lambda e: e.matmul(
            c.PS_B[:, 0:512], c.sel64_s, c.sums2b[:, 512:1024], start=False, stop=True),
            mark=True)
        ta_r = P.emit("A", lambda e: e.activation(c.recipb, c.PS_B[:, 0:512], AF.Copy),
                      deps=[tb_], mark=True)
        tv_o1 = P.emit("V", lambda e, opsum=opsum: e.tensor_tensor(
            c.oscr, opsum[0:64, :], c.recipb[0:64, :], op=OP.mult),
            deps=[ta_r], mark=True)
        ta_o = P.emit("A", lambda e, hb=hb: e.activation(
            c.o_hi[:, hb, :], c.oscr, AF.Copy), deps=[tv_o1, o_free[hb]], mark=True)
        tv_o2 = P.emit("V", lambda e, hb=hb: e.tensor_tensor(
            c.o_lo[:, hb, :], c.oscr, c.o_hi[:, hb, :], op=OP.subtract),
            deps=[ta_o], mark=True)
        ops_free[hb] = tv_o1
        psB_free = ta_r
        tTw = None
        for e8 in range(8):
            wop = c.PS_B[:, 512:1024]
            lh = c.wow[:, hb, 0, 128 * e8:128 * e8 + 128]
            llo = c.wow[:, hb, 1, 128 * e8:128 * e8 + 128]
            dep0 = [twl, ta_o, tv_o2, wo_ps_free]
            mm3(P, wop, lh, llo, c.o_hi[:, hb, :], c.o_lo[:, hb, :],
                True, True, deps=dep0)
            tTw = P.emit("T", lambda e: e.nop(), mark=True)
            if h == 0:
                if l == 0:
                    basef = c.xn_f[:, e8, 0:512]
                else:
                    basef = c.o1_f[:, e8, 1:513]
                x1_last = P.emit("V", lambda e, e8=e8, basef=basef, wop=wop:
                                 e.tensor_tensor(c.x1_f[:, e8, 1:513], wop, basef,
                                                 op=OP.add), deps=[tTw], mark=True)
            else:
                x1_last = P.emit("V", lambda e, e8=e8, wop=wop: e.tensor_tensor(
                    c.x1_f[:, e8, 1:513], wop, c.x1_f[:, e8, 1:513],
                    op=OP.add), deps=[tTw], mark=True)
            wo_ps_free = x1_last
        wow_free[hb] = tTw
        o_free[hb] = tTw
    c.free["PS_A"] = exp_done
    c.free["PS_B"] = psB_free
    c.free["PS_C"] = ops_free[0]
    c.free["PS_D"] = ops_free[1]
    c.free["wow0"] = wow_free[0]
    c.free["wow1"] = wow_free[1]
    c.free[f"kspill_r{l}"] = c.free.get("pv_prev")
    c.free["mvar"] = exp_done
    tcol = P.emit("V", lambda e: e.tensor_copy(
        c.x1col[:, l, :], c.x1_f[:, :, 512:513].rearrange("p a o -> p (a o)")),
        deps=[x1_last], mark=True)
    if mode == "x1":
        return

    ta = P.emit("A", lambda e: e.activation(c.x1s_hi, c.x1_f, AF.Copy),
                deps=[x1_last, P.tok("T")], mark=True)
    tv = P.emit("V", lambda e: e.tensor_tensor(c.x1s_lo, c.x1_f, c.x1s_hi,
                                               op=OP.subtract), deps=[ta], mark=True)
    nmm = 0
    sq_done = []
    tmm = None
    for d in range(8):
        off = 0 if d >= 4 else 1
        for dt in range(2):
            buf = c.x1s_hi if dt == 0 else c.x1s_lo
            P.emit("T", lambda e, buf=buf, d=d, off=off, nmm=nmm: e.matmul(
                c.PS_A[0:1, 0:512], c.ones_s[:, 0:1], buf[:, d, off:off + 512],
                start=(nmm == 0), stop=(nmm == 15)),
                deps=[tv, c.free.get("PS_A")] if nmm == 0 else ())
            nmm += 1
        xsl = c.x1_f[:, d, off:off + 512]
        tsq = P.emit("V", lambda e, xsl=xsl: e.tensor_tensor(
            c.gscr, xsl, xsl, op=OP.mult), deps=[x1_last, tmm, P.tok("A")],
            mark=True)
        tsh = P.emit("A", lambda e: e.activation(c.sqh, c.gscr, AF.Copy),
                     deps=[tsq], mark=True)
        tsl = P.emit("V", lambda e: e.tensor_tensor(c.sql, c.gscr, c.sqh,
                                                    op=OP.subtract), deps=[tsh],
                     mark=True)
        for dt in range(2):
            buf = c.sqh if dt == 0 else c.sql
            tmm = P.emit("T", lambda e, buf=buf, d=d, dt=dt: e.matmul(
                c.PS_A[0:1, 512:1024], c.ones_s[:, 0:1], buf,
                start=(d == 0 and dt == 0), stop=(d == 7 and dt == 1)),
                deps=[tsl], mark=True)
        sq_done.append(tsl)
    tstat = P.tok("T")
    inv = 1.0 / 1024.0
    P.emit("V", lambda e: e.tensor_scalar_mul(c.statr[0:1, 0, :], c.PS_A[0:1, 0:512],
                                              inv), deps=[tstat, P.tok("A")])
    P.emit("V", lambda e: e.tensor_scalar_mul(c.gscr[0:1, :],
                                              c.PS_A[0:1, 512:1024], inv))
    P.emit("V", lambda e: e.tensor_tensor(c.gscr2[0:1, :], c.statr[0:1, 0, :],
                                          c.statr[0:1, 0, :], op=OP.mult))
    P.vdrain()
    tvv = P.emit("V", lambda e: e.tensor_tensor(c.gscr[0:1, :], c.gscr[0:1, :],
                                                c.gscr2[0:1, :], op=OP.subtract),
                 mark=True)
    tas = P.emit("A", lambda e: e.activation(c.gscr2[0:1, :], c.gscr[0:1, :],
                                             AF.Sqrt, bias=c.eps_s[0:1, :]),
                 deps=[tvv], mark=True)
    P.emit("V", lambda e: e.reciprocal(c.statr[0:1, 1, :], c.gscr2[0:1, :]),
           deps=[tas])
    P.emit("V", lambda e: e.tensor_scalar(c.gscr[0:1, :], c.gscr[0:1, :],
                                          c.eps_s[0:1, :], None, op0=OP.add))
    P.emit("V", lambda e: e.tensor_tensor(c.gscr2[0:1, :], c.statr[0:1, 1, :],
                                          c.statr[0:1, 1, :], op=OP.mult))
    P.vdrain()
    P.emit("V", lambda e: e.tensor_tensor(c.gscr2[0:1, :], c.gscr[0:1, :],
                                          c.gscr2[0:1, :], op=OP.mult))
    P.vdrain()
    P.emit("V", lambda e: e.tensor_scalar(c.gscr2[0:1, :], c.gscr2[0:1, :],
                                          c.barsc[0:1, 3:4], c.barsc[0:1, 4:5],
                                          op0=OP.mult, op1=OP.add))
    P.vdrain()
    tvr = P.emit("V", lambda e: e.tensor_tensor(c.statr[0:1, 1, :],
                                                c.statr[0:1, 1, :],
                                                c.gscr2[0:1, :], op=OP.mult),
                 mark=True)
    tah = P.emit("A", lambda e: e.activation(c.statb[0:1, 0, 0:512],
                                             c.statr[0:1, 0, :], AF.Copy),
                 deps=[tvr], mark=True)
    tah2 = P.emit("A", lambda e: e.activation(c.statb[0:1, 0, 512:1024],
                                              c.statr[0:1, 1, :], AF.Copy), mark=True)
    tvl = P.emit("V", lambda e: e.tensor_tensor(c.statb[0:1, 1, 0:512],
                                                c.statr[0:1, 0, :],
                                                c.statb[0:1, 0, 0:512],
                                                op=OP.subtract), deps=[tah], mark=True)
    tvl2 = P.emit("V", lambda e: e.tensor_tensor(c.statb[0:1, 1, 512:1024],
                                                 c.statr[0:1, 1, :],
                                                 c.statb[0:1, 0, 512:1024],
                                                 op=OP.subtract), deps=[tah2],
                  mark=True)
    for hblk in range(2):
        P.emit("T", lambda e, hblk=hblk: e.matmul(
            c.PS_B[:, 512 * hblk:512 * hblk + 512], c.ones_s[0:1, :],
            c.statb[0:1, 0, 512 * hblk:512 * hblk + 512], start=True, stop=False),
            deps=[tvl2, tvl, c.free.get("PS_B")])
        P.emit("T", lambda e, hblk=hblk: e.matmul(
            c.PS_B[:, 512 * hblk:512 * hblk + 512], c.ones_s[0:1, :],
            c.statb[0:1, 1, 512 * hblk:512 * hblk + 512], start=False, stop=True))
    tbb = P.emit("T", lambda e: e.nop(), mark=True)
    h2_done = []
    tah = None
    for d in range(8):
        off = 0 if d >= 4 else 1
        xsl = c.x1_f[:, d, off:off + 512]
        P.emit("V", lambda e, xsl=xsl: e.tensor_tensor(
            c.gscr, xsl, c.PS_B[:, 0:512], op=OP.subtract),
            deps=[tbb, sq_done[-1], tah, P.tok("A")])
        P.emit("V", lambda e: e.tensor_tensor(c.gscr2, c.gscr, c.PS_B[:, 512:1024],
                                              op=OP.mult))
        tv3 = P.emit("V", lambda e, d=d: e.tensor_scalar(
            c.gscr, c.gscr2, c.lng_s[:, l, d:d + 1], c.lnb_s[:, l, d:d + 1],
            op0=OP.mult, op1=OP.add), mark=True)
        tah = P.emit("A", lambda e, d=d: e.activation(c.h2_hi[:, d, :], c.gscr,
                                                      AF.Copy), deps=[tv3], mark=True)
        h2_done.append(P.emit("V", lambda e, d=d: e.tensor_tensor(
            c.h2_lo[:, d, :], c.gscr, c.h2_hi[:, d, :], op=OP.subtract),
            deps=[tah], mark=True))
    h2tok = h2_done[-1]

    grps = [(0, 6), (6, 6), (12, 6), (18, 4)]
    w1b = 0
    w1_free = [c.free.get("w1w0"), c.free.get("w1w1")]
    ps_free = [c.free.get("PS_C"), c.free.get("PS_D")]
    w2_read = None
    h4_free = [None] * 6
    out_ev = [None, None]
    for (k0, kn) in grps:
        tw2h = P.dma("S", c.w2b_hi[:, 0:kn, :], c.hw2p[l, 128 * k0:128 * (k0 + kn), 0, :]
                     .rearrange("(cc p) e -> p cc e", p=128),
                     deps=[w2_read, tstat, P.tok("A"), P.tok("V")], stream="w2")
        tw2l = P.dma("S", c.w2b_lo[:, 0:kn, :], c.hw2p[l, 128 * k0:128 * (k0 + kn), 1, :]
                     .rearrange("(cc p) e -> p cc e", p=128), stream="w2")
        h4_done = []
        tvm_prev = None
        for mi in range(kn):
            m = k0 + mi
            for part, ps in ((0, c.PS_C), (1, c.PS_D)):
                tw1 = P.dma("S",
                            c.w1w[:, w1b].rearrange("p dc dt col -> p dc (dt col)"),
                            c.hw1p[l, 22 * part + m]
                            .rearrange("(dc p) dt col -> p dc (dt col)", p=128),
                            deps=[w1_free[w1b]], stream=f"w1{w1b}")
                for d in range(8):
                    first = (d == 0)
                    last = (d == 7)
                    dep0 = [tw1, h2tok, ps_free[part]] if first else ()
                    mm3(P, ps, c.w1w[:, w1b, d, 0, :], c.w1w[:, w1b, d, 1, :],
                        c.h2_hi[:, d, :], c.h2_lo[:, d, :], first, last, deps=dep0)
                tT = P.emit("T", lambda e: e.nop(), mark=True)
                w1_free[w1b] = tT
                w1b ^= 1
                if part == 0:
                    tta = tT
                else:
                    ttg = tT
            ta_a = P.emit("A", lambda e, m=m: e.activation(
                c.gscr2, c.PS_C, AF.Identity, bias=c.b1_s[:, l, 0, m:m + 1]),
                deps=[tta, tvm_prev], mark=True)
            ta_g = P.emit("A", lambda e, m=m: e.activation(
                c.gscr, c.PS_D, AF.Gelu, bias=c.b1_s[:, l, 1, m:m + 1]),
                deps=[ttg], mark=True)
            ps_free[0] = ta_a
            ps_free[1] = ta_g
            tv_m = P.emit("V", lambda e: e.tensor_tensor(c.gscr2, c.gscr2, c.gscr,
                                                         op=OP.mult),
                          deps=[ta_a, ta_g], mark=True)
            tvm_prev = tv_m
            ta_h = P.emit("A", lambda e, mi=mi: e.activation(
                c.h4_hi[:, mi, :], c.gscr2, AF.Copy),
                deps=[tv_m, h4_free[mi]], mark=True)
            tv_h = P.emit("V", lambda e, mi=mi: e.tensor_tensor(
                c.h4_lo[:, mi, :], c.gscr2, c.h4_hi[:, mi, :], op=OP.subtract),
                deps=[ta_h], mark=True)
            h4_done.append(tv_h)
        for e8 in range(8):
            pi = e8 % 2
            ps = c.PS_C if pi == 0 else c.PS_D
            for mi in range(kn):
                first = (mi == 0)
                last = (mi == kn - 1)
                dep0 = [tw2h, tw2l, h4_done[-1], ps_free[pi]] if first else ()
                mm3(P, ps, c.w2b_hi[:, mi, 128 * e8:128 * e8 + 128],
                    c.w2b_lo[:, mi, 128 * e8:128 * e8 + 128],
                    c.h4_hi[:, mi, :], c.h4_lo[:, mi, :], first, last, deps=dep0)
            tT = P.emit("T", lambda e: e.nop(), mark=True)
            w2_read = tT
            if l == 0:
                outv = c.o1_f[:, e8, 1:513]
            else:
                outv = c.xn_f[:, e8, 0:512]
            gidx = grps.index((k0, kn))
            if gidx == 0:
                tv = P.emit("V", lambda e, outv=outv, ps=ps, e8=e8: e.tensor_tensor(
                    outv, ps, c.x1_f[:, e8, 1:513], op=OP.add), deps=[tT], mark=True)
            elif gidx == len(grps) - 1:
                tv = P.emit("V", lambda e, outv=outv, ps=ps, e8=e8: e.scalar_tensor_tensor(
                    outv, ps, c.b2_s[:, l, e8:e8 + 1], outv, op0=OP.add, op1=OP.add),
                    deps=[tT], mark=True)
            else:
                tv = P.emit("V", lambda e, outv=outv, ps=ps: e.tensor_tensor(
                    outv, ps, outv, op=OP.add), deps=[tT], mark=True)
            ps_free[pi] = tv
            out_ev[pi] = tv
        for i in range(6):
            h4_free[i] = w2_read
    c.free["PS_C"] = ps_free[0]
    c.free["PS_D"] = ps_free[1]
    c.free["w1w0"] = w1_free[0]
    c.free["w1w1"] = w1_free[1]
    c.free["w2b"] = w2_read
    c.free["PS_A"] = tstat
    c.free["PS_B"] = h2_done[-1]
    if l == 0:
        P.emit("V", lambda e: e.tensor_copy(
            c.o1col, c.o1_f[:, :, 512:513].rearrange("p a o -> p (a o)")),
            deps=[out_ev[0], out_ev[1]], mark=True)
        c.free["o1_done"] = P.tok("V")
    else:
        c.free["out2_done"] = P.tok("V")


def emit_final(P, c, k):
    g = c.geo
    psA_free = c.free.get("PS_A")
    tv = None
    for t in range(4):
        for d in range(8):
            P.emit("T", lambda e, t=t, d=d: e.transpose(
                c.PS_A[:, 128 * d:128 * d + 128], c.xn_f[:, d, 128 * t:128 * t + 128],
                c.identf_s),
                deps=[c.free.get("out2_done"), psA_free] if d == 0 else ())
        tT = P.emit("T", lambda e: e.nop(), mark=True)
        psv = c.PS_A[:, 0:1024]
        tv = P.emit("V", lambda e, t=t, psv=psv: e.tensor_tensor(
            c.acc[:, t, :], c.acc[:, t, :], psv, op=OP.add),
            deps=[tT, c.free.get("acc_ev")], mark=True)
        psA_free = tv
    c.free["PS_A"] = psA_free
    tst = P.dma("G", c.out[512 * k:512 * k + 512, :].rearrange(
        "(a p) e -> p a e", p=128), c.acc, deps=[tv], stream="out")
    c.free["acc"] = tst


def emit_all(P, c):
    g = c.geo
    emit_startup(P, c)
    limit = globals().get("STRIPE_LIMIT")
    stripes = g.stripes if limit is None else g.stripes[:limit]
    for k in stripes:
        P.barrier()
        toff = g.t0_off(k)
        tabs_tok = None
        if not globals().get("NO_TABS"):
            for i, tbl in enumerate((c.Aq, c.Bq, c.Ak, c.Bk)):
                tabs_tok = P.dma("S", c.tabs[:, :, i, :],
                                 tbl[toff:toff + 512, :].rearrange("(a p) s -> p a s", p=128),
                                 stream="tabs")
        c.free["tabs_tok"] = tabs_tok
        emit_ph0(P, c, k)
        if globals().get("PH0_ONLY"):
            continue
        if k >= 0:
            emit_ph1(P, c, k)
        if not globals().get("NO_FFN"):
            emit_ffn(P, c, k, 0, "kv" if k == -g.warm else "full")
            if k >= -2:
                emit_ffn(P, c, k, 1,
                         {-2: "kv", -1: "x1"}.get(k, "full"))
        if k >= 0:
            emit_final(P, c, k)
    P.barrier(with_dma=True)



_CACHE = {}


def _get_program(geo_key):
    if geo_key not in _CACHE:
        geo = Geo(*geo_key)
        nc, P, c = build_program(geo)
        emit_all(P, c)
        P.finalize_block()
        _CACHE[geo_key] = (geo, nc)
    return _CACHE[geo_key]


def kernel(**inputs):
    x = np.asarray(inputs["L0_attn_query"])
    B, T, D = x.shape
    geo, nc = _get_program((T, B, 8))
    in_maps = stage_inputs(geo, inputs)
    res = run_bass_kernel_spmd(nc, in_maps, core_ids=list(range(geo.n_cores)))
    out = np.empty((B, T, D), dtype=np.float32)
    for ci in range(geo.n_cores):
        b, s = geo.core_pos(ci)
        out[b, s:s + geo.own] = res.results[ci]["out"]
    return out
